# version 01
# baseline (speedup 1.0000x reference)
import contextlib
import numpy as np
import concourse.bass as bass
import concourse.mybir as mybir
from concourse.bass_utils import run_bass_kernel_spmd
from concourse.bass_core_rust import IndirectOffsetOnAxis


class Alloc:

    def __init__(self, nc, es):
        self.nc, self.es = nc, es

    def sb(self, name, shape, dtype=mybir.dt.float32):
        t = self.es.enter_context(self.nc.sbuf_tensor(name, shape, dtype))
        setattr(self, name, t)
        return t

    def ps(self, name, shape, dtype=mybir.dt.float32):
        t = self.es.enter_context(self.nc.psum_tensor(name, shape, dtype))
        setattr(self, name, t)
        return t

F32 = mybir.dt.float32
I32 = mybir.dt.int32
AF = mybir.ActivationFunctionType
ALU = mybir.AluOpType
AX = mybir.AxisListType.X

NCORE = 8
D = 1024
T = 16384
TC = T // NCORE
H = 16
DH = 64
CCAP = 640
E = 32
EC = E * CCAP
LN_EPS = 1e-5
RMS_EPS = float(np.finfo(np.float64).eps)


class Sems:

    def __init__(self, nc, names):
        self.nc = nc
        self.h = {}
        self.c = {}
        self.ctx = []
        for n in names:
            hh = nc.alloc_semaphore(n)
            self.h[n] = hh
            self.c[n] = 0

    def inc(self, ins, name, n=1):
        self.c[name] += n
        return ins.then_inc(self.h[name], n)

    def wait(self, eng, name, val=None):
        eng.wait_ge(self.h[name], self.c[name] if val is None else val)

    def clear_then_barrier(self, block):
        nc = self.nc
        ksr = nc._kernel_sem_range
        n_extra = 2 + (1 if nc._bir_kernel_barrier_sem is not None else 0) + len(nc._monotonic_sems)
        rng = range(ksr.start + n_extra, ksr.stop)
        nc.gpsimd.dma_reset(rng)
        nc.gpsimd.sem_clear(rng)
        nc.all_engine_barrier()


def emit_ln(v, sc, sm, xt, out, width, mu, m2, var, rstd, mrs, sq, eps, vsem, ssem):
    @v
    def _(vector):
        vector.reduce_sum(mu[:], xt, axis=AX)
        vector.tensor_mul(sq[:], xt, xt)
        vector.reduce_sum(m2[:], sq[:], axis=AX)
        vector.drain()
        vector.tensor_scalar_mul(mu[:], mu[:], 1.0 / width)
        vector.tensor_scalar_mul(m2[:], m2[:], 1.0 / width)
        vector.drain()
        vector.tensor_tensor(var[:], mu[:], mu[:], ALU.mult)
        vector.drain()
        vector.tensor_sub(var[:], m2[:], var[:])
        vector.drain()
        vector.tensor_scalar_add(var[:], var[:], eps)
        vector.drain()
        sm.inc(vector.reciprocal(var[:], var[:]), vsem)

    @sc
    def _(scalar):
        sm.wait(scalar, vsem)
        sm.inc(scalar.activation(rstd[:], var[:], AF.Sqrt), ssem)

    @v
    def _(vector):
        sm.wait(vector, ssem)
        vector.tensor_tensor(mrs[:], mu[:], rstd[:], ALU.mult)
        vector.drain()
        sm.inc(vector.scalar_tensor_tensor(
            out, xt, rstd[:], mrs[:].to_broadcast(out.shape),
            ALU.mult, ALU.subtract), vsem)


def build_l1():
    nc = bass.Bass("TRN2", target_bir_lowering=False, detect_race_conditions=False)
    x = nc.dram_tensor("x", [TC, D], F32, kind="ExternalInput").ap()
    WT = nc.dram_tensor("WT", [D, D], F32, kind="ExternalInput").ap()
    knw = nc.dram_tensor("knw", [D], F32, kind="ExternalInput").ap()
    static = nc.dram_tensor("static", [TC, D], F32, kind="ExternalOutput").ap()
    kout = nc.dram_tensor("kout", [TC, D], F32, kind="ExternalOutput").ap()
    ident_d = nc.inline_tensor(np.eye(128, dtype=np.float32), "identc_d").ap()

    NT = TC // 128
    with contextlib.ExitStack() as es:
        a = Alloc(nc, es)
        for nm, sh in [("xt", [128, D]), ("xln", [128, D]), ("xlnT", [128, D]),
                       ("wtile", [128, 8 * D]), ("ident", [128, 128]), ("knwt", [128, D]),
                       ("stat", [128, D]), ("ktile", [128, D]), ("mu", [128, 1]),
                       ("m2", [128, 1]), ("var", [128, 1]), ("rstd", [128, 1]),
                       ("mrs", [128, 1]), ("sq", [128, D]), ("hm", [128, H]), ("hr", [128, H])]:
            a.sb(nm, sh)
        for nm, sh in [("pT", [128, 128]), ("pA", [128, 512]), ("pB", [128, 512])]:
            a.ps(nm, sh)
        xt, xln, xlnT, wtile, ident, knwt, stat, ktile = (
            a.xt, a.xln, a.xlnT, a.wtile, a.ident, a.knwt, a.stat, a.ktile)
        mu, m2, var, rstd, mrs, sq, hm, hr = a.mu, a.m2, a.var, a.rstd, a.mrs, a.sq, a.hm, a.hr
        pT, pA, pB = a.pT, a.pA, a.pB
        block = es.enter_context(nc.Block())
        sm = Sems(nc, ["sin", "sv", "ss", "stn", "sout"])
        sm.clear_then_barrier(block)

        @block.sync
        def _(sync):
            sm.inc(sync.dma_start(out=wtile.ap().rearrange("p (c o) -> p c o", c=8),
                                  in_=WT.rearrange("(c p) o -> p c o", p=128)), "sin", 16)
            sm.inc(sync.dma_start(out=ident[:], in_=ident_d[:, :]), "sin", 16)
            sm.inc(sync.dma_start(out=knwt[:], in_=knw[None, :].to_broadcast([128, D])), "sin", 16)

        for i in range(NT):
            vafter_prev_use = sm.c["stn"]

            @block.sync
            def _(sync, i=i, vprev=vafter_prev_use):
                if i >= 1:
                    sm.wait(sync, "sv")
                sm.inc(sync.dma_start(out=xt[:], in_=x[i * 128:(i + 1) * 128, :]), "sin", 16)

            @block.vector
            def _(vector, i=i):
                sm.wait(vector, "sin")
                if i >= 1:
                    sm.wait(vector, "stn")
                    sm.wait(vector, "sout")

            emit_ln(block.vector, block.scalar, sm, xt[:], xln[:], D,
                    mu, m2, var, rstd, mrs, sq, LN_EPS, "sv", "ss")

            @block.vector
            def _(vector, i=i):
                vector.drain()
                vector.tensor_mul(sq[:], xln[:], xln[:])
                vector.drain()
                vector.reduce_sum(hm.ap().unsqueeze(2),
                                  sq.ap().rearrange("p (h d) -> p h d", h=H), axis=AX)
                vector.drain()
                vector.tensor_scalar(hm[:], hm[:], 1.0 / DH, RMS_EPS, ALU.mult, ALU.add)
                vector.drain()
                sm.inc(vector.reciprocal(hm[:], hm[:]), "sv")

            @block.scalar
            def _(scalar):
                sm.wait(scalar, "sv")
                sm.inc(scalar.activation(hr[:], hm[:], AF.Sqrt), "ss")

            @block.vector
            def _(vector):
                sm.wait(vector, "ss")
                vector.tensor_tensor(
                    ktile.ap().rearrange("p (h d) -> p h d", h=H),
                    xln.ap().rearrange("p (h d) -> p h d", h=H),
                    hr.ap().unsqueeze(2).to_broadcast([128, H, DH]),
                    ALU.mult)
                vector.drain()
                sm.inc(vector.tensor_tensor(ktile[:], ktile[:], knwt[:], ALU.mult), "sv")

            kdone = sm.c["sv"]

            @block.gpsimd
            def _(gpsimd, kd=kdone):
                sm.wait(gpsimd, "sv", kd)
                sm.inc(gpsimd.dma_start(out=kout[i * 128:(i + 1) * 128, :], in_=ktile[:]), "sout", 16)

            for c in range(8):
                @block.tensor
                def _(tensor, i=i, c=c):
                    if i == 0 and c == 0:
                        sm.wait(tensor, "sin")
                    sm.wait(tensor, "sv")
                    sm.wait(tensor, "ss")
                    sm.inc(tensor.transpose(pT[:, 0:128], xln[:, c * 128:(c + 1) * 128], ident[:]), "stn")

                @block.scalar
                def _(scalar, c=c):
                    sm.wait(scalar, "stn")
                    sm.inc(scalar.copy(xlnT[:, c * 128:(c + 1) * 128], pT[:]), "ss")

            @block.tensor
            def _(tensor, i=i):
                sm.wait(tensor, "ss")
                if i >= 1:
                    sm.wait(tensor, "sv")
                for half, pb in ((0, pA), (1, pB)):
                    for c in range(8):
                        ins = tensor.matmul(
                            pb[:], xlnT[:, c * 128:(c + 1) * 128],
                            wtile[:, c * D + half * 512: c * D + half * 512 + 512],
                            start=(c == 0), stop=(c == 7))
                sm.inc(ins, "stn")

            @block.vector
            def _(vector, i=i):
                sm.wait(vector, "stn")
                if i >= 1:
                    sm.wait(vector, "sout")
                vector.tensor_copy(stat[:, 0:512], pA[:])
                sm.inc(vector.tensor_copy(stat[:, 512:1024], pB[:]), "sv")

            @block.gpsimd
            def _(gpsimd, i=i):
                sm.wait(gpsimd, "sv")
                sm.inc(gpsimd.dma_start(out=static[i * 128:(i + 1) * 128, :], in_=stat[:]), "sout", 16)

        @block.gpsimd
        def _(gpsimd):
            sm.wait(gpsimd, "sout")
    return nc


def build_l2(nch=64):
    nc = bass.Bass("TRN2", target_bir_lowering=False, detect_race_conditions=False)
    k4 = nc.dram_tensor("k4", [nch, 128, 4, DH], F32, kind="ExternalInput").ap()
    v4 = nc.dram_tensor("v4", [nch, 128, 4, DH], F32, kind="ExternalInput").ap()
    lam4 = nc.dram_tensor("lam4", [4], F32, kind="ExternalInput").ap()
    eta4 = nc.dram_tensor("eta4", [4], F32, kind="ExternalInput").ap()
    o4 = nc.dram_tensor("o4", [nch, 128, 4, DH], F32, kind="ExternalOutput").ap()

    ii, jj = np.meshgrid(np.arange(128), np.arange(128), indexing="ij")
    iotacl_d = nc.inline_tensor(np.maximum(ii - jj, 0).astype(np.float32), "iotacl_d").ap()
    strict_d = nc.inline_tensor((jj < ii).astype(np.float32), "strict01_d").ap()
    ident_d = nc.inline_tensor(np.eye(128, dtype=np.float32), "identc_d").ap()
    ioi_d = nc.inline_tensor((np.arange(128, dtype=np.float32) + 1).reshape(128, 1), "ioi_d").ap()
    ior_d = nc.inline_tensor((127.0 - np.arange(128, dtype=np.float32)).reshape(128, 1), "ior_d").ap()
    c128_d = nc.inline_tensor(np.full((64, 1), 128.0, dtype=np.float32), "c128_d").ap()
    ones1_d = nc.inline_tensor(np.ones((1, 128), dtype=np.float32), "ones1_d").ap()

    NLEV = 7
    with contextlib.ExitStack() as es:
        a = Alloc(nc, es)
        for nm, sh in [("iotact", [128, 128]), ("strict", [128, 128]), ("identt", [128, 128]),
                       ("ioi", [128, 1]), ("ior", [128, 1]), ("c128", [64, 1]), ("ones1", [1, 128]),
                       ("vals", [1, 16]), ("tmp4", [1, 4]), ("bct", [128, 16]),
                       ("kk", [128, DH]), ("av", [128, 1]), ("gv", [128, 1])]:
            a.sb(nm, sh)
        for b in range(2):
            a.sb(f"kv{b}", [128, 4 * DH])
            a.sb(f"vv{b}", [128, 4 * DH])
            a.sb(f"ot{b}", [128, 4 * DH])
        for s in range(4):
            a.sb(f"mskL{s}", [128, 128]); a.sb(f"mskU{s}", [128, 128])
            a.sb(f"MUa{s}", [128, 192]); a.sb(f"MUb{s}", [128, 192])
            a.sb(f"Mta{s}", [128, 128]); a.sb(f"Mtb{s}", [128, 128])
            a.sb(f"kdT{s}", [64, 128]); a.sb(f"Ubf{s}", [128, DH]); a.sb(f"Ks{s}", [128, DH])
            a.sb(f"Sa{s}", [64, DH]); a.sb(f"Sb{s}", [64, DH])
            a.sb(f"nli{s}", [128, 1]); a.sb(f"bkr{s}", [128, 1]); a.sb(f"lC{s}", [64, 1])
        a.ps("pTR", [64, 128]); a.ps("pA", [128, 128]); a.ps("pW", [128, DH])
        a.ps("pF", [128, 192]); a.ps("pMt", [128, 128]); a.ps("pS", [64, DH])
        a.ps("pBC", [128, 16])
        block = es.enter_context(nc.Block())
        sm = Sems(nc, ["sin", "sv", "ss", "stn", "sout"])
        sm.clear_then_barrier(block)

        @block.sync
        def _(sync):
            for dst, src in [(a.iotact, iotacl_d), (a.strict, strict_d), (a.identt, ident_d)]:
                sm.inc(sync.dma_start(out=dst[:], in_=src[:, :]), "sin", 16)
            sm.inc(sync.dma_start(out=a.ioi[:], in_=ioi_d[:, :]), "sin", 16)
            sm.inc(sync.dma_start(out=a.ior[:], in_=ior_d[:, :]), "sin", 16)
            sm.inc(sync.dma_start(out=a.c128[:], in_=c128_d[:, :]), "sin", 16)
            sm.inc(sync.dma_start(out=a.ones1[:], in_=ones1_d[:, :]), "sin", 16)
            sm.inc(sync.dma_start(out=a.vals[:, 0:4], in_=lam4[None, :]), "sin", 16)
            sm.inc(sync.dma_start(out=a.tmp4[:], in_=eta4[None, :]), "sin", 16)
            sm.inc(sync.dma_start(out=a.kv0.ap().rearrange("p (s d) -> p s d", s=4),
                                  in_=k4[0]), "sin", 16)
            sm.inc(sync.dma_start(out=a.vv0.ap().rearrange("p (s d) -> p s d", s=4),
                                  in_=v4[0]), "sin", 16)

        @block.scalar
        def _(scalar):
            sm.wait(scalar, "sin")
            scalar.activation(a.vals[:, 0:4], a.vals[:, 0:4], AF.Sigmoid)
            scalar.drain()
            scalar.activation(a.tmp4[:], a.tmp4[:], AF.Tanh)
            scalar.drain()
            sm.inc(scalar.activation(a.vals[:, 4:8], a.vals[:, 0:4], AF.Ln), "ss")

        @block.vector
        def _(vector):
            sm.wait(vector, "ss")
            vector.tensor_scalar(a.vals[:, 8:12], a.vals[:, 0:4], -1.0, 1.0, ALU.mult, ALU.add)
            vector.drain()
            vector.tensor_mul(a.vals[:, 8:12], a.vals[:, 8:12], a.tmp4[:])
            vector.drain()
            vector.tensor_scalar_mul(a.vals[:, 12:16], a.vals[:, 8:12], 1.0 / DH)
            vector.drain()
            sm.inc(vector.tensor_scalar_mul(a.vals[:, 8:12], a.vals[:, 8:12], -1.0 / DH), "sv")

        @block.tensor
        def _(tensor):
            sm.wait(tensor, "sv")
            sm.wait(tensor, "sin")
            sm.inc(tensor.matmul(a.pBC[:], a.ones1[:], a.vals[:], start=True, stop=True), "stn")

        @block.scalar
        def _(scalar):
            sm.wait(scalar, "stn")
            sm.inc(scalar.copy(a.bct[:], a.pBC[:]), "ss")

        for s in range(4):
            lnl = a.bct[:, 4 + s:5 + s]

            @block.scalar
            def _(scalar, s=s, lnl=lnl):
                sm.wait(scalar, "ss")
                scalar.activation(getattr(a, f"mskL{s}")[:], a.iotact[:], AF.Exp, scale=lnl)
                scalar.activation(getattr(a, f"nli{s}")[:], a.ioi[:], AF.Exp, scale=lnl)
                scalar.activation(getattr(a, f"bkr{s}")[:], a.ior[:], AF.Exp, scale=lnl)
                sm.inc(scalar.activation(getattr(a, f"lC{s}")[:], a.c128[:, 0:1][0:64, :], AF.Exp,
                                         scale=lnl[0:64, :]), "ss")
                scalar.drain()
                scalar.mul(getattr(a, f"nli{s}")[:], getattr(a, f"nli{s}")[:], -1.0)
                sm.inc(scalar.mul(getattr(a, f"bkr{s}")[:], getattr(a, f"bkr{s}")[:],
                                  a.bct[:, 8 + s:9 + s]), "ss")

            @block.vector
            def _(vector, s=s):
                sm.wait(vector, "ss")
                vector.tensor_tensor(getattr(a, f"mskL{s}")[:], getattr(a, f"mskL{s}")[:],
                                     a.strict[:], ALU.mult)
                vector.drain()
                vector.tensor_scalar(getattr(a, f"mskL{s}")[:], getattr(a, f"mskL{s}")[:],
                                     a.bct[:, 12 + s:13 + s], None, ALU.mult)
                vector.memset(getattr(a, f"Sa{s}")[:], 0.0)
                sm.inc(vector.memset(getattr(a, f"Sb{s}")[:], 0.0), "sv")

            @block.tensor
            def _(tensor, s=s):
                sm.wait(tensor, "sv")
                sm.inc(tensor.transpose(a.pMt[:], getattr(a, f"mskL{s}")[:], a.identt[:]), "stn")

            @block.scalar
            def _(scalar, s=s):
                sm.wait(scalar, "stn")
                sm.inc(scalar.copy(getattr(a, f"mskU{s}")[:], a.pMt[:]), "ss")

        last = {"pTR_read": 0, "pA_read": 0, "pW_read": 0, "pF_read_s": 0, "pF_read_v": 0,
                "pMt_read": 0, "pS_read": 0, "kv_read": [0, 0], "ot_st": [0, 0],
                "kvv_read": [0, 0], "kvs_read": [0, 0]}
        load_done = [0] * (nch + 1)
        load_done[0] = sm.c["sin"]
        sdone = [0] * 4
        sdone_mm = [0] * 4

        for c in range(nch):
            buf = c % 2
            kv = getattr(a, f"kv{buf}"); vv = getattr(a, f"vv{buf}"); ot = getattr(a, f"ot{buf}")

            if c + 1 < nch:
                nbuf = (c + 1) % 2

                @block.sync
                def _(sync, c=c, nbuf=nbuf):
                    sm.wait(sync, "stn", last["kv_read"][nbuf])
                    sm.wait(sync, "sv", last["kvv_read"][nbuf])
                    sm.wait(sync, "ss", last["kvs_read"][nbuf])
                    sm.inc(sync.dma_start(out=getattr(a, f"kv{nbuf}").ap().rearrange("p (s d) -> p s d", s=4),
                                          in_=k4[c + 1]), "sin", 16)
                    sm.inc(sync.dma_start(out=getattr(a, f"vv{nbuf}").ap().rearrange("p (s d) -> p s d", s=4),
                                          in_=v4[c + 1]), "sin", 16)

                load_done[c + 1] = sm.c["sin"]

            for s in range(4):
                mskL = getattr(a, f"mskL{s}"); mskU = getattr(a, f"mskU{s}")
                MU = [getattr(a, f"MUa{s}"), getattr(a, f"MUb{s}")]
                Mt = [getattr(a, f"Mta{s}"), getattr(a, f"Mtb{s}")]
                kdT = getattr(a, f"kdT{s}"); Ubf = getattr(a, f"Ubf{s}"); Ks = getattr(a, f"Ks{s}")
                Scur = getattr(a, f"Sa{s}") if c % 2 == 0 else getattr(a, f"Sb{s}")
                Snxt = getattr(a, f"Sb{s}") if c % 2 == 0 else getattr(a, f"Sa{s}")
                kap = kv[:, s * DH:(s + 1) * DH]
                vap = vv[:, s * DH:(s + 1) * DH]

                @block.tensor
                def _(tensor, c=c, s=s):
                    sm.wait(tensor, "sin", load_done[c])
                    sm.wait(tensor, "ss", last["pTR_read"])
                    sm.inc(tensor.transpose(a.pTR[:], kap, a.identt[:]), "stn")

                @block.scalar
                def _(scalar):
                    sm.wait(scalar, "stn")
                    sm.inc(scalar.copy(kdT[:], a.pTR[:]), "ss")

                last["pTR_read"] = sm.c["ss"]
                t_kdT_done = sm.c["ss"]

                @block.tensor
                def _(tensor, c=c, s=s):
                    sm.wait(tensor, "ss", t_kdT_done)
                    sm.wait(tensor, "sv", last["pA_read"])
                    sm.inc(tensor.matmul(a.pA[:], kdT[:], kdT[:], start=True, stop=True), "stn")
                    sm.wait(tensor, "sv", max(last["pW_read"], sdone[s]))
                    sm.inc(tensor.matmul(a.pW[:], kdT[:], Scur[:], start=True, stop=True), "stn")

                t_AW_done = sm.c["stn"]

                @block.vector
                def _(vector, c=c, s=s):
                    sm.wait(vector, "stn", t_AW_done)
                    vector.tensor_tensor(MU[0][:, 0:128], a.pA[:], mskL[:], ALU.mult)
                    vector.tensor_tensor(Mt[0][:], a.pA[:], mskU[:], ALU.mult)
                    sm.inc(vector.scalar_tensor_tensor(
                        MU[0][:, 128:192], a.pW[:], getattr(a, f"nli{s}")[:],
                        vap, ALU.mult, ALU.add), "sv")

                last["pA_read"] = sm.c["sv"]
                last["pW_read"] = sm.c["sv"]
                v_m0_done = sm.c["sv"]

                @block.vector
                def _(vector, c=c, s=s):
                    vector.tensor_mul(a.kk[:], kap, kap)
                    vector.drain()
                    vector.reduce_sum(a.av[:], a.kk[:], axis=AX)
                    vector.drain()
                    sm.inc(vector.tensor_scalar(a.gv[:], a.av[:], a.bct[:, 12 + s:13 + s], 1.0,
                                                ALU.mult, ALU.add), "sv")

                v_g_done = sm.c["sv"]

                @block.scalar
                def _(scalar, c=c, s=s):
                    sm.wait(scalar, "stn", sdone_mm[s])
                    sm.inc(scalar.mul(Ks[:], kap, getattr(a, f"bkr{s}")[:]), "ss")

                s_ks_done = sm.c["ss"]

                vprev = v_m0_done
                s_mt_ready = None
                for lev in range(NLEV):
                    cur = lev % 2
                    nxt = 1 - cur
                    if lev < NLEV - 1:
                        @block.tensor
                        def _(tensor, lev=lev, cur=cur, vprev=vprev, smt=s_mt_ready):
                            sm.wait(tensor, "sv", max(vprev, last["pF_read_v"]))
                            if smt is not None:
                                sm.wait(tensor, "ss", max(smt, last["pF_read_s"]))
                            else:
                                sm.wait(tensor, "ss", last["pF_read_s"])
                            sm.inc(tensor.matmul(a.pF[:], Mt[cur][:], MU[cur][:], start=True, stop=True), "stn")

                        t_f = sm.c["stn"]

                        @block.scalar
                        def _(scalar, cur=cur, nxt=nxt, t_f=t_f):
                            sm.wait(scalar, "stn", t_f)
                            sm.inc(scalar.copy(MU[nxt][:, 0:128], a.pF[:, 0:128]), "ss")

                        s_m = sm.c["ss"]
                        last["pF_read_s"] = s_m

                        @block.vector
                        def _(vector, cur=cur, nxt=nxt, t_f=t_f):
                            sm.wait(vector, "stn", t_f)
                            sm.inc(vector.tensor_add(MU[nxt][:, 128:192], MU[cur][:, 128:192],
                                                     a.pF[:, 128:192]), "sv")

                        vprev = sm.c["sv"]
                        last["pF_read_v"] = sm.c["sv"]

                        @block.tensor
                        def _(tensor, nxt=nxt, s_m=s_m):
                            sm.wait(tensor, "ss", max(s_m, last["pMt_read"]))
                            sm.inc(tensor.transpose(a.pMt[:], MU[nxt][:, 0:128], a.identt[:]), "stn")

                        t_mt = sm.c["stn"]

                        @block.scalar
                        def _(scalar, nxt=nxt, t_mt=t_mt):
                            sm.wait(scalar, "stn", t_mt)
                            sm.inc(scalar.copy(Mt[nxt][:], a.pMt[:]), "ss")

                        last["pMt_read"] = sm.c["ss"]
                        s_mt_ready = sm.c["ss"]
                    else:
                        @block.tensor
                        def _(tensor, cur=cur, vprev=vprev, smt=s_mt_ready):
                            sm.wait(tensor, "sv", max(vprev, last["pW_read"]))
                            if smt is not None:
                                sm.wait(tensor, "ss", smt)
                            sm.inc(tensor.matmul(a.pW[:], Mt[cur][:], MU[cur][:, 128:192],
                                                 start=True, stop=True), "stn")

                        t_f = sm.c["stn"]

                        @block.vector
                        def _(vector, cur=cur, t_f=t_f):
                            sm.wait(vector, "stn", t_f)
                            sm.inc(vector.tensor_add(Ubf[:], MU[cur][:, 128:192], a.pW[:]), "sv")

                        last["pW_read"] = sm.c["sv"]
                        v_ubf = sm.c["sv"]

                @block.vector
                def _(vector, c=c, s=s, v_ubf=v_ubf, v_g=v_g_done):
                    sm.wait(vector, "sv", max(v_ubf, v_g))
                    if c >= 2:
                        sm.wait(vector, "sout", last["ot_st"][buf])
                    sm.inc(vector.scalar_tensor_tensor(
                        ot[:, s * DH:(s + 1) * DH], Ubf[:], a.gv[:], vap,
                        ALU.mult, ALU.subtract), "sv")

                v_o_done = sm.c["sv"]

                @block.tensor
                def _(tensor, s_ks=s_ks_done, v_ubf=v_ubf):
                    sm.wait(tensor, "ss", s_ks)
                    sm.wait(tensor, "sv", v_ubf)
                    sm.wait(tensor, "sv", last["pS_read"])
                    sm.inc(tensor.matmul(a.pS[:], Ks[:], Ubf[:], start=True, stop=True), "stn")

                t_s = sm.c["stn"]
                sdone_mm[s] = t_s

                @block.vector
                def _(vector, t_s=t_s, s=s):
                    sm.wait(vector, "stn", t_s)
                    sm.inc(vector.scalar_tensor_tensor(
                        Snxt[:], Scur[:], getattr(a, f"lC{s}")[:], a.pS[:],
                        ALU.mult, ALU.add), "sv")

                last["pS_read"] = sm.c["sv"]
                sdone[s] = sm.c["sv"]

            v_all_o = sm.c["sv"]

            @block.gpsimd
            def _(gpsimd, c=c, buf=buf, v_all_o=v_all_o):
                sm.wait(gpsimd, "sv", v_all_o)
                sm.inc(gpsimd.dma_start(out=o4[c], in_=ot.ap().rearrange("p (s d) -> p s d", s=4)),
                       "sout", 16)

            last["ot_st"][buf] = sm.c["sout"]
            last["kv_read"][buf] = sm.c["stn"]
            last["kvv_read"][buf] = sm.c["sv"]
            last["kvs_read"][buf] = sm.c["ss"]

        @block.gpsimd
        def _(gpsimd):
            sm.wait(gpsimd, "sout")
    return nc


def build_l3():
    nc = bass.Bass("TRN2", target_bir_lowering=False, detect_race_conditions=False)
    x = nc.dram_tensor("x", [TC, D], F32, kind="ExternalInput").ap()
    st = nc.dram_tensor("st", [TC, D], F32, kind="ExternalInput").ap()
    od = nc.dram_tensor("od", [TC, D], F32, kind="ExternalInput").ap()
    ow = nc.dram_tensor("ow", [D], F32, kind="ExternalInput").ap()
    ob = nc.dram_tensor("ob", [D], F32, kind="ExternalInput").ap()
    wr = nc.dram_tensor("wr", [D, E], F32, kind="ExternalInput").ap()
    resid = nc.dram_tensor("resid", [TC, D], F32, kind="ExternalOutput").ap()
    xt2o = nc.dram_tensor("xt2o", [TC, D], F32, kind="ExternalOutput").ap()
    lgo = nc.dram_tensor("lgo", [TC, E], F32, kind="ExternalOutput").ap()
    ident_d = nc.inline_tensor(np.eye(128, dtype=np.float32), "identc_d").ap()

    NT = TC // 128
    with contextlib.ExitStack() as es:
        a = Alloc(nc, es)
        for nm, sh in [("xt", [128, D]), ("stt", [128, D]), ("odt", [128, D]),
                       ("z", [128, D]), ("lnb", [128, D]), ("xt2", [128, D]),
                       ("rez", [128, D]), ("xt2T", [128, D]), ("lgt", [128, E]),
                       ("ident", [128, 128]), ("owt", [128, D]), ("obt", [128, D]),
                       ("wrt", [128, 8 * E]), ("mu", [128, 1]), ("m2", [128, 1]),
                       ("var", [128, 1]), ("rstd", [128, 1]), ("mrs", [128, 1]),
                       ("sq", [128, D])]:
            a.sb(nm, sh)
        a.ps("pT", [128, 128]); a.ps("pL", [128, E])
        block = es.enter_context(nc.Block())
        sm = Sems(nc, ["sin", "sv", "ss", "stn", "sout"])
        sm.clear_then_barrier(block)

        @block.sync
        def _(sync):
            sm.inc(sync.dma_start(out=a.ident[:], in_=ident_d[:, :]), "sin", 16)
            sm.inc(sync.dma_start(out=a.owt[:], in_=ow[None, :].to_broadcast([128, D])), "sin", 16)
            sm.inc(sync.dma_start(out=a.obt[:], in_=ob[None, :].to_broadcast([128, D])), "sin", 16)
            sm.inc(sync.dma_start(out=a.wrt.ap().rearrange("p (c e) -> p c e", c=8),
                                  in_=wr.rearrange("(c p) e -> p c e", p=128)), "sin", 16)

        for i in range(NT):
            @block.sync
            def _(sync, i=i):
                if i >= 1:
                    sm.wait(sync, "sv")
                    sm.wait(sync, "stn")
                sm.inc(sync.dma_start(out=a.xt[:], in_=x[i * 128:(i + 1) * 128, :]), "sin", 16)
                sm.inc(sync.dma_start(out=a.stt[:], in_=st[i * 128:(i + 1) * 128, :]), "sin", 16)
                sm.inc(sync.dma_start(out=a.odt[:], in_=od[i * 128:(i + 1) * 128, :]), "sin", 16)

            @block.vector
            def _(vector, i=i):
                sm.wait(vector, "sin")
                if i >= 1:
                    sm.wait(vector, "sout")
                sm.inc(vector.tensor_add(a.z[:], a.stt[:], a.odt[:]), "sv")

            @block.vector
            def _(vector):
                vector.drain()
            emit_ln(block.vector, block.scalar, sm, a.z[:], a.lnb[:], D,
                    a.mu, a.m2, a.var, a.rstd, a.mrs, a.sq, LN_EPS, "sv", "ss")

            @block.vector
            def _(vector):
                vector.drain()
                vector.tensor_mul(a.lnb[:], a.lnb[:], a.owt[:])
                vector.drain()
                vector.tensor_add(a.lnb[:], a.lnb[:], a.obt[:])
                vector.drain()
                sm.inc(vector.tensor_add(a.rez[:], a.xt[:], a.lnb[:]), "sv")

            @block.vector
            def _(vector):
                vector.drain()
            emit_ln(block.vector, block.scalar, sm, a.rez[:], a.xt2[:], D,
                    a.mu, a.m2, a.var, a.rstd, a.mrs, a.sq, LN_EPS, "sv", "ss")
            v_xt2 = sm.c["sv"]

            for c in range(8):
                @block.tensor
                def _(tensor, c=c):
                    sm.wait(tensor, "sv")
                    sm.wait(tensor, "ss")
                    sm.wait(tensor, "sin")
                    sm.inc(tensor.transpose(a.pT[:], a.xt2[:, c * 128:(c + 1) * 128], a.ident[:]), "stn")

                @block.scalar
                def _(scalar, c=c):
                    sm.wait(scalar, "stn")
                    sm.inc(scalar.copy(a.xt2T[:, c * 128:(c + 1) * 128], a.pT[:]), "ss")

            @block.tensor
            def _(tensor, i=i):
                sm.wait(tensor, "ss")
                if i >= 1:
                    sm.wait(tensor, "sv")
                for c in range(8):
                    ins = tensor.matmul(a.pL[:], a.xt2T[:, c * 128:(c + 1) * 128],
                                        a.wrt[:, c * E:(c + 1) * E],
                                        start=(c == 0), stop=(c == 7))
                sm.inc(ins, "stn")

            @block.vector
            def _(vector):
                sm.wait(vector, "stn")
                sm.inc(vector.tensor_copy(a.lgt[:], a.pL[:]), "sv")

            v_all = sm.c["sv"]

            @block.gpsimd
            def _(gpsimd, i=i, v_all=v_all, v_xt2=v_xt2):
                sm.wait(gpsimd, "sv", v_all)
                sm.inc(gpsimd.dma_start(out=resid[i * 128:(i + 1) * 128, :], in_=a.rez[:]), "sout", 16)
                sm.inc(gpsimd.dma_start(out=xt2o[i * 128:(i + 1) * 128, :], in_=a.xt2[:]), "sout", 16)
                sm.inc(gpsimd.dma_start(out=lgo[i * 128:(i + 1) * 128, :], in_=a.lgt[:]), "sout", 16)

        @block.gpsimd
        def _(gpsimd):
            sm.wait(gpsimd, "sout")
    return nc


def build_l6():
    nc = bass.Bass("TRN2", target_bir_lowering=False, detect_race_conditions=False)
    rz = nc.dram_tensor("rz", [TC, D], F32, kind="ExternalInput").ap()
    xt2 = nc.dram_tensor("xt2", [TC, D], F32, kind="ExternalInput").ap()
    yfull = nc.dram_tensor("yfull", [EC, D], F32, kind="ExternalInput").ap()
    slot = nc.dram_tensor("slot", [TC], F32, kind="ExternalInput").ap()
    gate = nc.dram_tensor("gate", [TC], F32, kind="ExternalInput").ap()
    out = nc.dram_tensor("out", [TC, D], F32, kind="ExternalOutput").ap()

    NT = TC // 128
    with contextlib.ExitStack() as es:
        a = Alloc(nc, es)
        for nm, sh in [("rzt", [128, D]), ("xtt", [128, D]), ("yt", [128, D]),
                       ("ot", [128, D]), ("gt", [128, 1]), ("sl", [128, 1])]:
            a.sb(nm, sh)
        a.sb("sli", [128, 1], I32)
        block = es.enter_context(nc.Block())
        sm = Sems(nc, ["sin", "sv", "sg", "sout"])
        sm.clear_then_barrier(block)

        for i in range(NT):
            @block.sync
            def _(sync, i=i):
                if i >= 1:
                    sm.wait(sync, "sv")
                sm.inc(sync.dma_start(out=a.rzt[:], in_=rz[i * 128:(i + 1) * 128, :]), "sin", 16)
                sm.inc(sync.dma_start(out=a.xtt[:], in_=xt2[i * 128:(i + 1) * 128, :]), "sin", 16)
                sm.inc(sync.dma_start(out=a.gt[:], in_=gate[i * 128:(i + 1) * 128, None]), "sin", 16)
                sm.inc(sync.dma_start(out=a.sl[:], in_=slot[i * 128:(i + 1) * 128, None]), "sin", 16)

            @block.vector
            def _(vector, i=i):
                sm.wait(vector, "sin")
                if i >= 1:
                    sm.wait(vector, "sg")
                sm.inc(vector.tensor_copy(a.sli[:], a.sl[:]), "sv")

            @block.gpsimd
            def _(gpsimd, i=i):
                sm.wait(gpsimd, "sv")
                sm.inc(gpsimd.indirect_dma_start(
                    a.yt[:], None, yfull[:, :],
                    IndirectOffsetOnAxis(a.sli[:], 0),
                    bounds_check=EC - 1, oob_is_err=False), "sg", 16)

            @block.vector
            def _(vector, i=i):
                sm.wait(vector, "sg")
                if i >= 1:
                    sm.wait(vector, "sout")
                vector.scalar_tensor_tensor(a.ot[:], a.yt[:], a.gt[:], a.rzt[:],
                                            ALU.mult, ALU.add)
                vector.drain()
                sm.inc(vector.tensor_add(a.ot[:], a.ot[:], a.xtt[:]), "sv")

            @block.gpsimd
            def _(gpsimd, i=i):
                sm.wait(gpsimd, "sv")
                sm.inc(gpsimd.dma_start(out=out[i * 128:(i + 1) * 128, :], in_=a.ot[:]), "sout", 16)

        @block.gpsimd
        def _(gpsimd):
            sm.wait(gpsimd, "sout")
    return nc


def build_l5():
    nc = bass.Bass("TRN2", target_bir_lowering=False, detect_race_conditions=False)
    lg_d = nc.dram_tensor("lg", [T, E], F32, kind="ExternalInput").ap()
    xt_d = nc.dram_tensor("xt", [T, D], F32, kind="ExternalInput").ap()
    wg_d = nc.dram_tensor("wg4", [4, D, D], F32, kind="ExternalInput").ap()
    wu_d = nc.dram_tensor("wu4", [4, D, D], F32, kind="ExternalInput").ap()
    wd_d = nc.dram_tensor("wd4", [4, D, D], F32, kind="ExternalInput").ap()
    soff_d = nc.dram_tensor("soff", [128, 1], F32, kind="ExternalInput").ap()
    y4_d = nc.dram_tensor("y4", [4 * CCAP, D], F32, kind="ExternalOutput").ap()
    slot_d = nc.dram_tensor("slot", [T], F32, kind="ExternalOutput").ap()
    gate_d = nc.dram_tensor("gate", [T], F32, kind="ExternalOutput").ap()
    inv_d = nc.dram_tensor("inv", [4 * CCAP, 1], I32, kind="Internal").ap()

    ident_d = nc.inline_tensor(np.eye(128, dtype=np.float32), "identc_d").ap()
    ones_d = nc.inline_tensor(np.ones((128, 128), dtype=np.float32), "ones_d").ap()
    ioe_d = nc.inline_tensor(np.tile(np.arange(E, dtype=np.float32), (128, 1)), "ioe_d").ap()
    ioe100_d = nc.inline_tensor(np.tile(np.arange(E, dtype=np.float32) - 100.0, (128, 1)), "ioe100_d").ap()
    tokc_np = (np.arange(128)[:, None] * 128 + np.arange(128)[None, :]).astype(np.int32)
    tokc_d = nc.inline_tensor(tokc_np, "tokc_d").ap()

    NF = T // 128 // 128
    with contextlib.ExitStack() as es:
        a = Alloc(nc, es)
        for p in range(4):
            a.sb(f"pool{p}", [128, 5120])
        for nm, sh in [("ident", [128, 128]), ("ones", [128, 128]), ("ioe", [128, E]),
                       ("ioe100", [128, E]), ("lmax", [128, 128]), ("ssum", [128, 128]),
                       ("gatee", [128, 128]), ("top1", [128, 128]), ("tot", [128, E]),
                       ("totT", [32, 128]), ("cum", [32, 128]), ("excl", [32, 128]),
                       ("off", [128, E]), ("posi", [128, 128]), ("acc", [128, 128]),
                       ("slotf", [128, 128]), ("slotl", [128, 128]), ("sofft", [128, 1]), ("acc2", [128, 128]),
                       ("wbuf", [128, 8 * D]), ("btile", [128, D]), ("ytile", [128, D])]:
            a.sb(nm, sh)
        a.sb("tokc", [128, 128], I32)
        a.sb("pad", [128, 20], I32)
        a.sb("sli", [128, 128], I32)
        a.sb("gidx", [128, 1], I32)
        a.ps("pTs", [128, 128]); a.ps("pTr", [32, 128]); a.ps("pTr2", [128, 32])
        a.ps("pm0", [128, 320]); a.ps("pm1", [128, 320])
        a.ps("pm2", [128, 320]); a.ps("pm3", [128, 320])
        block = es.enter_context(nc.Block())
        sm = Sems(nc, ["sin", "sv", "ss", "stn", "sg", "sout"])
        sm.clear_then_barrier(block)

        lgv = a.pool0.ap()[:, 0:4096].rearrange("p (f e) -> p f e", e=E)
        ohv = a.pool1.ap()[:, 0:4096].rearrange("p (e f) -> p e f", e=E)
        cntv = a.pool2.ap()[:, 0:4096].rearrange("p (e f) -> p e f", e=E)
        t1v = a.pool3.ap()[:, 0:4096].rearrange("p (e f) -> p e f", e=E)

        @block.sync
        def _(sync):
            for dst, src in [(a.ident, ident_d), (a.ones, ones_d), (a.ioe, ioe_d),
                             (a.ioe100, ioe100_d), (a.tokc, tokc_d), (a.sofft, soff_d)]:
                sm.inc(sync.dma_start(out=dst[:], in_=src[:, :]), "sin", 16)
            sm.inc(sync.dma_start(out=lgv, in_=lg_d.rearrange("(p f) e -> p f e", p=128)), "sin", 16)

        @block.vector
        def _(vector):
            sm.wait(vector, "sin")
            vector.reduce_max(a.lmax[:], lgv, axis=AX)
            vector.drain()
            sm.inc(vector.tensor_tensor(
                a.pool1.ap()[:, 0:4096].rearrange("p (f e) -> p f e", e=E), lgv,
                a.lmax.ap().unsqueeze(2).to_broadcast([128, 128, E]), ALU.subtract), "sv")

        @block.scalar
        def _(scalar):
            sm.wait(scalar, "sv")
            sm.inc(scalar.activation(a.pool1[:, 0:4096], a.pool1[:, 0:4096], AF.Exp), "ss")

        @block.vector
        def _(vector):
            sm.wait(vector, "ss")
            vector.reduce_sum(a.ssum.ap().unsqueeze(2),
                              a.pool1.ap()[:, 0:4096].rearrange("p (f e) -> p f e", e=E), axis=AX)
            vector.drain()
            vector.reciprocal(a.gatee[:], a.ssum[:])
            vector.tensor_tensor(
                a.pool1.ap()[:, 0:4096].rearrange("p (f e) -> p f e", e=E), lgv,
                a.lmax.ap().unsqueeze(2).to_broadcast([128, 128, E]), ALU.is_ge)
            vector.drain()
            vector.tensor_tensor(
                a.pool1.ap()[:, 0:4096].rearrange("p (f e) -> p f e", e=E),
                a.pool1.ap()[:, 0:4096].rearrange("p (f e) -> p f e", e=E),
                a.ioe100.ap().unsqueeze(1).to_broadcast([128, 128, E]), ALU.mult)
            vector.drain()
            vector.tensor_reduce(a.top1.ap().unsqueeze(2),
                                 a.pool1.ap()[:, 0:4096].rearrange("p (f e) -> p f e", e=E),
                                 AX, ALU.min)
            vector.drain()
            vector.tensor_scalar_add(a.top1[:], a.top1[:], 100.0)
            vector.drain()
            sm.inc(vector.tensor_tensor(
                ohv, a.top1.ap().unsqueeze(1).to_broadcast([128, E, 128]),
                a.ioe.ap().unsqueeze(2).to_broadcast([128, E, 128]), ALU.is_equal), "sv")

        @block.vector
        def _(vector):
            for e in range(E):
                vector.tensor_tensor_scan(cntv[:, e, :], a.ones[:], ohv[:, e, :],
                                          0.0, ALU.mult, ALU.add)
            vector.drain()
            sm.inc(vector.tensor_copy(a.tot.ap().unsqueeze(2), cntv[:, :, 127:128]), "sv")

        @block.tensor
        def _(tensor):
            sm.wait(tensor, "sv")
            sm.wait(tensor, "sin")
            sm.inc(tensor.transpose(a.pTr[:], a.tot[:], a.ident[:]), "stn")

        @block.scalar
        def _(scalar):
            sm.wait(scalar, "stn")
            sm.inc(scalar.copy(a.totT[:], a.pTr[:]), "ss")

        @block.vector
        def _(vector):
            sm.wait(vector, "ss")
            vector.tensor_tensor_scan(a.cum[:], a.ones[0:32, :], a.totT[:], 0.0,
                                      ALU.mult, ALU.add)
            vector.drain()
            vector.memset(a.excl[:, 0:1], 0.0)
            vector.tensor_copy(a.excl[:, 1:128], a.cum[:, 0:127])
            sm.inc(vector.drain(), "sv")

        @block.tensor
        def _(tensor):
            sm.wait(tensor, "sv")
            sm.inc(tensor.transpose(a.pTr2[:], a.excl[:], a.ident[0:32, 0:32]), "stn")

        @block.scalar
        def _(scalar):
            sm.wait(scalar, "stn")
            sm.inc(scalar.copy(a.off[:], a.pTr2[:]), "ss")

        @block.vector
        def _(vector):
            sm.wait(vector, "ss")
            vector.tensor_tensor(t1v, cntv,
                                 a.off.ap().unsqueeze(2).to_broadcast([128, E, 128]), ALU.add)
            vector.drain()
            vector.tensor_tensor(t1v, t1v, ohv, ALU.mult)
            vector.drain()
            vector.tensor_reduce(a.posi.ap().unsqueeze(2),
                                 a.pool3.ap()[:, 0:4096].rearrange("p (e f) -> p e f", e=E).transpose([0, 2, 1]),
                                 AX, ALU.add)
            vector.drain()
            vector.tensor_scalar(a.acc[:], a.posi[:], 641.0, None, ALU.is_lt)
            vector.scalar_tensor_tensor(a.slotf[:], a.top1[:], 640.0, a.posi[:],
                                        ALU.mult, ALU.add)
            vector.drain()
            vector.tensor_scalar_add(a.slotf[:], a.slotf[:], -20481.0)
            vector.drain()
            vector.tensor_mul(a.slotf[:], a.slotf[:], a.acc[:])
            vector.drain()
            vector.tensor_scalar_add(a.slotf[:], a.slotf[:], 20480.0)
            vector.tensor_mul(a.gatee[:], a.gatee[:], a.acc[:])
            vector.drain()
            vector.scalar_tensor_tensor(a.slotl[:], a.sofft[:].to_broadcast([128, 128]),
                                        -1.0, a.slotf[:], ALU.mult, ALU.add)
            vector.drain()
            vector.tensor_scalar(a.acc2[:], a.slotl[:], 0.0, None, ALU.is_lt)
            vector.drain()
            vector.scalar_tensor_tensor(a.slotl[:], a.acc2[:], 30000.0, a.slotl[:],
                                        ALU.mult, ALU.add)
            vector.drain()
            vector.tensor_copy(a.sli[:], a.slotl[:])
            sm.inc(vector.drain(), "sv")

        v_route = sm.c["sv"]

        @block.gpsimd
        def _(gpsimd):
            sm.wait(gpsimd, "sv", v_route)
            sm.inc(gpsimd.dma_start(out=slot_d.rearrange("(p f) -> p f", p=128), in_=a.slotf[:]), "sout", 16)
            sm.inc(gpsimd.dma_start(out=gate_d.rearrange("(p f) -> p f", p=128), in_=a.gatee[:]), "sout", 16)
            gpsimd.memset(a.pad[:], 16384)
            sm.inc(gpsimd.dma_start(out=inv_d.rearrange("(p g) one -> p (g one)", p=128),
                                    in_=a.pad[:]), "sg", 16)
            sm.wait(gpsimd, "sg")
            bc = gpsimd.to_reg(4 * CCAP - 1)
            for j in range(128):
                sm.inc(gpsimd.indirect_dma_start(
                    inv_d[:, :], IndirectOffsetOnAxis(a.sli[:, j:j + 1], 0),
                    a.tokc[:, j:j + 1], None,
                    bounds_check=bc, oob_is_err=False), "sg", 16)
            sm.wait(gpsimd, "sg")

        g_scat = sm.c["sg"]
        bcg = [None]

        @block.gpsimd
        def _(gpsimd):
            bcg[0] = gpsimd.to_reg(T - 1)

        sv_next_wait = 0
        for e in range(4):
            @block.sync
            def _(sync, e=e):
                if e > 0:
                    sm.wait(sync, "stn")
                sm.inc(sync.dma_start(out=a.wbuf.ap().rearrange("p (hc f) -> p hc f", hc=8),
                                      in_=wg_d[e].rearrange("(hc p) f -> p hc f", p=128)), "sin", 16)

            w_g = sm.c["sin"]

            for t in range(5):
                @block.gpsimd
                def _(gpsimd, e=e, t=t):
                    sm.wait(gpsimd, "sg", g_scat)
                    if not (e == 0 and t == 0):
                        sm.wait(gpsimd, "stn")
                    sm.inc(gpsimd.dma_start(out=a.gidx[:], in_=inv_d[e * CCAP + t * 128:e * CCAP + (t + 1) * 128, :]), "sg", 16)
                    sm.wait(gpsimd, "sg")
                    sm.inc(gpsimd.indirect_dma_start(
                        a.btile[:], None, xt_d[:, :],
                        IndirectOffsetOnAxis(a.gidx[:], 0),
                        bounds_check=bcg[0], oob_is_err=False), "sg", 16)

                for hc in range(8):
                    @block.tensor
                    def _(tensor, t=t, hc=hc):
                        sm.wait(tensor, "sg")
                        sm.wait(tensor, "ss")
                        sm.inc(tensor.transpose(a.pTs[:], a.btile[:, hc * 128:(hc + 1) * 128],
                                                a.ident[:]), "stn")

                    @block.scalar
                    def _(scalar, t=t, hc=hc):
                        sm.wait(scalar, "stn")
                        sm.inc(scalar.copy(a.pool0[:, hc * 640 + t * 128: hc * 640 + (t + 1) * 128],
                                           a.pTs[:]), "ss")

            s_bufT = sm.c["ss"]

            for fc in range(8):
                for half, pm in ((0, a.pm0), (1, a.pm1)):
                    @block.tensor
                    def _(tensor, e=e, fc=fc, half=half, pm=pm, w_g=w_g):
                        sm.wait(tensor, "sin", w_g)
                        sm.wait(tensor, "ss")
                        sm.wait(tensor, "sv")
                        for hc in range(8):
                            ins = tensor.matmul(
                                pm[:], a.wbuf[:, hc * D + fc * 128: hc * D + fc * 128 + 128],
                                a.pool0[:, hc * 640 + half * 320: hc * 640 + half * 320 + 320],
                                start=(hc == 0), stop=(hc == 7))
                        sm.inc(ins, "stn")

                    @block.scalar
                    def _(scalar, fc=fc, half=half, pm=pm):
                        sm.wait(scalar, "stn")
                        sm.inc(scalar.activation(
                            a.pool1[:, fc * 640 + half * 320: fc * 640 + half * 320 + 320],
                            pm[:], AF.Silu), "ss")

            @block.sync
            def _(sync, e=e):
                sm.wait(sync, "stn")
                sm.inc(sync.dma_start(out=a.wbuf.ap().rearrange("p (hc f) -> p hc f", hc=8),
                                      in_=wu_d[e].rearrange("(hc p) f -> p hc f", p=128)), "sin", 16)

            w_u = sm.c["sin"]
            for fc in range(8):
                for half, pm in ((0, a.pm2), (1, a.pm3)):
                    @block.tensor
                    def _(tensor, e=e, fc=fc, half=half, pm=pm, w_u=w_u):
                        sm.wait(tensor, "sin", w_u)
                        sm.wait(tensor, "sv")
                        for hc in range(8):
                            ins = tensor.matmul(
                                pm[:], a.wbuf[:, hc * D + fc * 128: hc * D + fc * 128 + 128],
                                a.pool0[:, hc * 640 + half * 320: hc * 640 + half * 320 + 320],
                                start=(hc == 0), stop=(hc == 7))
                        sm.inc(ins, "stn")

                    @block.vector
                    def _(vector, fc=fc, half=half, pm=pm):
                        sm.wait(vector, "stn")
                        sm.wait(vector, "ss")
                        sl = slice(fc * 640 + half * 320, fc * 640 + half * 320 + 320)
                        sm.inc(vector.tensor_tensor(a.pool2[:, sl], a.pool1[:, sl], pm[:],
                                                    ALU.mult), "sv")

            @block.sync
            def _(sync, e=e):
                sm.wait(sync, "stn")
                sm.inc(sync.dma_start(out=a.wbuf.ap().rearrange("p (hc f) -> p hc f", hc=8),
                                      in_=wd_d[e].rearrange("(fc p) h -> p fc h", p=128)), "sin", 16)

            w_d = sm.c["sin"]
            for hc in range(8):
                for half, pm in ((0, a.pm0), (1, a.pm1)):
                    @block.tensor
                    def _(tensor, e=e, hc=hc, half=half, pm=pm, w_d=w_d):
                        sm.wait(tensor, "sin", w_d)
                        sm.wait(tensor, "sv")
                        sm.wait(tensor, "ss")
                        for fc in range(8):
                            ins = tensor.matmul(
                                pm[:], a.wbuf[:, fc * D + hc * 128: fc * D + hc * 128 + 128],
                                a.pool2[:, fc * 640 + half * 320: fc * 640 + half * 320 + 320],
                                start=(fc == 0), stop=(fc == 7))
                        sm.inc(ins, "stn")

                    @block.scalar
                    def _(scalar, hc=hc, half=half, pm=pm):
                        sm.wait(scalar, "stn")
                        sm.inc(scalar.copy(
                            a.pool3[:, hc * 640 + half * 320: hc * 640 + half * 320 + 320],
                            pm[:]), "ss")

            s_yT = sm.c["ss"]

            for t in range(5):
                for hc in range(8):
                    @block.tensor
                    def _(tensor, t=t, hc=hc, s_yT=s_yT):
                        sm.wait(tensor, "ss", s_yT)
                        sm.wait(tensor, "ss")
                        if t + hc == 0:
                            sm.wait(tensor, "sout")
                        sm.inc(tensor.transpose(a.pTs[:], a.pool3[:, hc * 640 + t * 128: hc * 640 + (t + 1) * 128],
                                                a.ident[:]), "stn")

                    @block.scalar
                    def _(scalar, t=t, hc=hc):
                        sm.wait(scalar, "stn")
                        if hc == 0:
                            sm.wait(scalar, "sout")
                        sm.inc(scalar.copy(a.ytile[:, hc * 128:(hc + 1) * 128], a.pTs[:]), "ss")

                @block.gpsimd
                def _(gpsimd, e=e, t=t):
                    sm.wait(gpsimd, "ss")
                    sm.inc(gpsimd.dma_start(
                        out=y4_d[e * CCAP + t * 128: e * CCAP + (t + 1) * 128, :],
                        in_=a.ytile[:]), "sout", 16)

        @block.gpsimd
        def _(gpsimd):
            sm.wait(gpsimd, "sout")
    return nc


def kernel(**inputs):
    cores = list(range(NCORE))
    x = np.ascontiguousarray(inputs["L0_attn_x"].reshape(T, D))
    WT = np.ascontiguousarray(inputs["L0_attn_W"].T)
    knw = np.ascontiguousarray(np.tile(inputs["L0_attn_k_norm_w"], H))

    nc1 = build_l1()
    maps = [{"x": x[c * TC:(c + 1) * TC], "WT": WT, "knw": knw} for c in cores]
    r1 = run_bass_kernel_spmd(nc1, maps, core_ids=cores)
    static = np.concatenate([r["static"] for r in r1.results], axis=0)
    kmat = np.concatenate([r["kout"] for r in r1.results], axis=0)

    k_str = kmat.reshape(2, 8192, H, DH).transpose(0, 2, 1, 3).reshape(2 * H, 8192, DH)
    v_str = static.reshape(2, 8192, H, DH).transpose(0, 2, 1, 3).reshape(2 * H, 8192, DH)
    lam_l = inputs["L0_attn_lam_logit"]
    eta_r = inputs["L0_attn_eta_raw"]
    nc2 = build_l2(nch=64)
    maps = []
    for c in cores:
        sl = slice(c * 4, (c + 1) * 4)
        heads = [(c * 4 + i) % H for i in range(4)]
        maps.append({
            "k4": np.ascontiguousarray(k_str[sl].reshape(4, 64, 128, DH).transpose(1, 2, 0, 3)),
            "v4": np.ascontiguousarray(v_str[sl].reshape(4, 64, 128, DH).transpose(1, 2, 0, 3)),
            "lam4": np.ascontiguousarray(lam_l[heads]),
            "eta4": np.ascontiguousarray(eta_r[heads]),
        })
    r2 = run_bass_kernel_spmd(nc2, maps, core_ids=cores)
    o_str = np.stack([r["o4"].transpose(2, 0, 1, 3).reshape(4, 8192, DH)
                      for r in r2.results]).reshape(2 * H, 8192, DH)
    o_dyn = np.ascontiguousarray(
        o_str.reshape(2, H, 8192, DH).transpose(0, 2, 1, 3).reshape(T, D))

    nc3 = build_l3()
    maps = [{"x": x[c * TC:(c + 1) * TC], "st": static[c * TC:(c + 1) * TC],
             "od": o_dyn[c * TC:(c + 1) * TC], "ow": inputs["L0_attn_o_norm_w"],
             "ob": inputs["L0_attn_o_norm_b"], "wr": inputs["L0_ffn_w_router"]}
            for c in cores]
    r3 = run_bass_kernel_spmd(nc3, maps, core_ids=cores)
    resid1 = np.concatenate([r["resid"] for r in r3.results], axis=0)
    xt2 = np.concatenate([r["xt2o"] for r in r3.results], axis=0)
    logits = np.concatenate([r["lgo"] for r in r3.results], axis=0)

    nc5 = build_l5()
    wg, wu, wd = inputs["L0_ffn_w_gate"], inputs["L0_ffn_w_up"], inputs["L0_ffn_w_down"]
    maps = [{"lg": logits, "xt": xt2,
             "wg4": np.ascontiguousarray(wg[c * 4:(c + 1) * 4]),
             "wu4": np.ascontiguousarray(wu[c * 4:(c + 1) * 4]),
             "wd4": np.ascontiguousarray(wd[c * 4:(c + 1) * 4]),
             "soff": np.full((128, 1), c * 4 * CCAP, dtype=np.float32)}
            for c in cores]
    r5 = run_bass_kernel_spmd(nc5, maps, core_ids=cores)
    yfull = np.concatenate([r["y4"] for r in r5.results], axis=0)
    slot = r5.results[0]["slot"]
    gate = r5.results[0]["gate"]

    nc6 = build_l6()
    maps = [{"rz": resid1[c * TC:(c + 1) * TC], "xt2": xt2[c * TC:(c + 1) * TC],
             "yfull": yfull, "slot": slot[c * TC:(c + 1) * TC],
             "gate": gate[c * TC:(c + 1) * TC]} for c in cores]
    r6 = run_bass_kernel_spmd(nc6, maps, core_ids=cores)
    out = np.concatenate([r["out"] for r in r6.results], axis=0)
    return out.reshape(2, 8192, D)


# version 02
# speedup vs baseline: 1.8526x; 1.8526x over previous
import contextlib
import numpy as np
import concourse.bass as bass
import concourse.mybir as mybir
from concourse.bass_utils import run_bass_kernel_spmd
from concourse.bass_core_rust import IndirectOffsetOnAxis


class Alloc:

    def __init__(self, nc, es):
        self.nc, self.es = nc, es

    def sb(self, name, shape, dtype=mybir.dt.float32):
        t = self.es.enter_context(self.nc.sbuf_tensor(name, shape, dtype))
        setattr(self, name, t)
        return t

    def ps(self, name, shape, dtype=mybir.dt.float32):
        t = self.es.enter_context(self.nc.psum_tensor(name, shape, dtype))
        setattr(self, name, t)
        return t

F32 = mybir.dt.float32
I32 = mybir.dt.int32
AF = mybir.ActivationFunctionType
ALU = mybir.AluOpType
AX = mybir.AxisListType.X

NCORE = 8
D = 1024
T = 16384
TC = T // NCORE
H = 16
DH = 64
CCAP = 640
E = 32
EC = E * CCAP
LN_EPS = 1e-5
RMS_EPS = float(np.finfo(np.float64).eps)


class Sems:

    def __init__(self, nc, names):
        self.nc = nc
        self.h = {}
        self.c = {}
        self.ctx = []
        for n in names:
            hh = nc.alloc_semaphore(n)
            self.h[n] = hh
            self.c[n] = 0

    def inc(self, ins, name, n=1):
        self.c[name] += n
        return ins.then_inc(self.h[name], n)

    def wait(self, eng, name, val=None):
        eng.wait_ge(self.h[name], self.c[name] if val is None else val)

    def clear_then_barrier(self, block):
        nc = self.nc
        ksr = nc._kernel_sem_range
        n_extra = 2 + (1 if nc._bir_kernel_barrier_sem is not None else 0) + len(nc._monotonic_sems)
        rng = range(ksr.start + n_extra, ksr.stop)
        nc.gpsimd.dma_reset(rng)
        nc.gpsimd.sem_clear(rng)
        nc.all_engine_barrier()


def emit_ln(v, sc, sm, xt, out, width, mu, m2, var, rstd, mrs, sq, eps, vsem, ssem):
    @v
    def _(vector):
        vector.reduce_sum(mu[:], xt, axis=AX)
        vector.tensor_mul(sq[:], xt, xt)
        vector.reduce_sum(m2[:], sq[:], axis=AX)
        vector.drain()
        vector.tensor_scalar_mul(mu[:], mu[:], 1.0 / width)
        vector.tensor_scalar_mul(m2[:], m2[:], 1.0 / width)
        vector.drain()
        vector.tensor_tensor(var[:], mu[:], mu[:], ALU.mult)
        vector.drain()
        vector.tensor_sub(var[:], m2[:], var[:])
        vector.drain()
        vector.tensor_scalar_add(var[:], var[:], eps)
        vector.drain()
        sm.inc(vector.reciprocal(var[:], var[:]), vsem)

    @sc
    def _(scalar):
        sm.wait(scalar, vsem)
        sm.inc(scalar.activation(rstd[:], var[:], AF.Sqrt), ssem)

    @v
    def _(vector):
        sm.wait(vector, ssem)
        vector.tensor_tensor(mrs[:], mu[:], rstd[:], ALU.mult)
        vector.drain()
        sm.inc(vector.scalar_tensor_tensor(
            out, xt, rstd[:], mrs[:].to_broadcast(out.shape),
            ALU.mult, ALU.subtract), vsem)


def build_l1():
    nc = bass.Bass("TRN2", target_bir_lowering=False, detect_race_conditions=False)
    x = nc.dram_tensor("x", [TC, D], F32, kind="ExternalInput").ap()
    WT = nc.dram_tensor("WT", [D, D], F32, kind="ExternalInput").ap()
    knw = nc.dram_tensor("knw", [D], F32, kind="ExternalInput").ap()
    static = nc.dram_tensor("static", [TC, D], F32, kind="ExternalOutput").ap()
    kout = nc.dram_tensor("kout", [TC, D], F32, kind="ExternalOutput").ap()
    ident_d = nc.inline_tensor(np.eye(128, dtype=np.float32), "identc_d").ap()

    NT = TC // 128
    with contextlib.ExitStack() as es:
        a = Alloc(nc, es)
        for nm, sh in [("xt", [128, D]), ("xln", [128, D]), ("xlnT", [128, D]),
                       ("wtile", [128, 8 * D]), ("ident", [128, 128]), ("knwt", [128, D]),
                       ("stat", [128, D]), ("ktile", [128, D]), ("mu", [128, 1]),
                       ("m2", [128, 1]), ("var", [128, 1]), ("rstd", [128, 1]),
                       ("mrs", [128, 1]), ("sq", [128, D]), ("hm", [128, H]), ("hr", [128, H])]:
            a.sb(nm, sh)
        for nm, sh in [("pT", [128, 128]), ("pA", [128, 512]), ("pB", [128, 512])]:
            a.ps(nm, sh)
        xt, xln, xlnT, wtile, ident, knwt, stat, ktile = (
            a.xt, a.xln, a.xlnT, a.wtile, a.ident, a.knwt, a.stat, a.ktile)
        mu, m2, var, rstd, mrs, sq, hm, hr = a.mu, a.m2, a.var, a.rstd, a.mrs, a.sq, a.hm, a.hr
        pT, pA, pB = a.pT, a.pA, a.pB
        block = es.enter_context(nc.Block())
        sm = Sems(nc, ["sin", "sv", "ss", "stn", "sout"])
        sm.clear_then_barrier(block)

        @block.sync
        def _(sync):
            sm.inc(sync.dma_start(out=wtile.ap().rearrange("p (c o) -> p c o", c=8),
                                  in_=WT.rearrange("(c p) o -> p c o", p=128)), "sin", 16)
            sm.inc(sync.dma_start(out=ident[:], in_=ident_d[:, :]), "sin", 16)
            sm.inc(sync.dma_start(out=knwt[:], in_=knw[None, :].to_broadcast([128, D])), "sin", 16)

        for i in range(NT):
            vafter_prev_use = sm.c["stn"]

            @block.sync
            def _(sync, i=i, vprev=vafter_prev_use):
                if i >= 1:
                    sm.wait(sync, "sv")
                sm.inc(sync.dma_start(out=xt[:], in_=x[i * 128:(i + 1) * 128, :]), "sin", 16)

            @block.vector
            def _(vector, i=i):
                sm.wait(vector, "sin")
                if i >= 1:
                    sm.wait(vector, "stn")
                    sm.wait(vector, "sout")

            emit_ln(block.vector, block.scalar, sm, xt[:], xln[:], D,
                    mu, m2, var, rstd, mrs, sq, LN_EPS, "sv", "ss")

            @block.vector
            def _(vector, i=i):
                vector.drain()
                vector.tensor_mul(sq[:], xln[:], xln[:])
                vector.drain()
                vector.reduce_sum(hm.ap().unsqueeze(2),
                                  sq.ap().rearrange("p (h d) -> p h d", h=H), axis=AX)
                vector.drain()
                vector.tensor_scalar(hm[:], hm[:], 1.0 / DH, RMS_EPS, ALU.mult, ALU.add)
                vector.drain()
                sm.inc(vector.reciprocal(hm[:], hm[:]), "sv")

            @block.scalar
            def _(scalar):
                sm.wait(scalar, "sv")
                sm.inc(scalar.activation(hr[:], hm[:], AF.Sqrt), "ss")

            @block.vector
            def _(vector):
                sm.wait(vector, "ss")
                vector.tensor_tensor(
                    ktile.ap().rearrange("p (h d) -> p h d", h=H),
                    xln.ap().rearrange("p (h d) -> p h d", h=H),
                    hr.ap().unsqueeze(2).to_broadcast([128, H, DH]),
                    ALU.mult)
                vector.drain()
                sm.inc(vector.tensor_tensor(ktile[:], ktile[:], knwt[:], ALU.mult), "sv")

            kdone = sm.c["sv"]

            @block.gpsimd
            def _(gpsimd, kd=kdone):
                sm.wait(gpsimd, "sv", kd)
                sm.inc(gpsimd.dma_start(out=kout[i * 128:(i + 1) * 128, :], in_=ktile[:]), "sout", 16)

            for c in range(8):
                @block.tensor
                def _(tensor, i=i, c=c):
                    if i == 0 and c == 0:
                        sm.wait(tensor, "sin")
                    sm.wait(tensor, "sv")
                    sm.wait(tensor, "ss")
                    sm.inc(tensor.transpose(pT[:, 0:128], xln[:, c * 128:(c + 1) * 128], ident[:]), "stn")

                @block.scalar
                def _(scalar, c=c):
                    sm.wait(scalar, "stn")
                    sm.inc(scalar.copy(xlnT[:, c * 128:(c + 1) * 128], pT[:]), "ss")

            @block.tensor
            def _(tensor, i=i):
                sm.wait(tensor, "ss")
                if i >= 1:
                    sm.wait(tensor, "sv")
                for half, pb in ((0, pA), (1, pB)):
                    for c in range(8):
                        ins = tensor.matmul(
                            pb[:], xlnT[:, c * 128:(c + 1) * 128],
                            wtile[:, c * D + half * 512: c * D + half * 512 + 512],
                            start=(c == 0), stop=(c == 7))
                sm.inc(ins, "stn")

            @block.vector
            def _(vector, i=i):
                sm.wait(vector, "stn")
                if i >= 1:
                    sm.wait(vector, "sout")
                vector.tensor_copy(stat[:, 0:512], pA[:])
                sm.inc(vector.tensor_copy(stat[:, 512:1024], pB[:]), "sv")

            @block.gpsimd
            def _(gpsimd, i=i):
                sm.wait(gpsimd, "sv")
                sm.inc(gpsimd.dma_start(out=static[i * 128:(i + 1) * 128, :], in_=stat[:]), "sout", 16)

        @block.gpsimd
        def _(gpsimd):
            sm.wait(gpsimd, "sout")
    return nc


def build_l2(nch=64):
    nc = bass.Bass("TRN2", target_bir_lowering=False, detect_race_conditions=False)
    k4 = nc.dram_tensor("k4", [nch, 128, 4, DH], F32, kind="ExternalInput").ap()
    v4 = nc.dram_tensor("v4", [nch, 128, 4, DH], F32, kind="ExternalInput").ap()
    lam4 = nc.dram_tensor("lam4", [4], F32, kind="ExternalInput").ap()
    eta4 = nc.dram_tensor("eta4", [4], F32, kind="ExternalInput").ap()
    o4 = nc.dram_tensor("o4", [nch, 128, 4, DH], F32, kind="ExternalOutput").ap()

    ii, jj = np.meshgrid(np.arange(128), np.arange(128), indexing="ij")
    iotacl_d = nc.inline_tensor(np.maximum(ii - jj, 0).astype(np.float32), "iotacl_d").ap()
    strict_d = nc.inline_tensor((jj < ii).astype(np.float32), "strict01_d").ap()
    ident_d = nc.inline_tensor(np.eye(128, dtype=np.float32), "identc_d").ap()
    ioi_d = nc.inline_tensor((np.arange(128, dtype=np.float32) + 1).reshape(128, 1), "ioi_d").ap()
    ior_d = nc.inline_tensor((127.0 - np.arange(128, dtype=np.float32)).reshape(128, 1), "ior_d").ap()
    c128_d = nc.inline_tensor(np.full((64, 1), 128.0, dtype=np.float32), "c128_d").ap()
    ones1_d = nc.inline_tensor(np.ones((1, 128), dtype=np.float32), "ones1_d").ap()

    NLEV = 7
    with contextlib.ExitStack() as es:
        a = Alloc(nc, es)
        for nm, sh in [("iotact", [128, 128]), ("strict", [128, 128]), ("identt", [128, 128]),
                       ("ioi", [128, 1]), ("ior", [128, 1]), ("c128", [64, 1]), ("ones1", [1, 128]),
                       ("vals", [1, 16]), ("tmp4", [1, 4]), ("bct", [128, 16]),
                       ("kk", [128, DH]), ("av", [128, 1])]:
            a.sb(nm, sh)
        for b in range(2):
            a.sb(f"kv{b}", [128, 4 * DH])
            a.sb(f"vv{b}", [128, 4 * DH])
            a.sb(f"ot{b}", [128, 4 * DH])
        for s in range(4):
            a.sb(f"mskL{s}", [128, 128]); a.sb(f"mskU{s}", [128, 128])
            a.sb(f"MUa{s}", [128, 192]); a.sb(f"MUb{s}", [128, 192])
            a.sb(f"Mta{s}", [128, 128]); a.sb(f"Mtb{s}", [128, 128])
            a.sb(f"kdT{s}", [64, 128]); a.sb(f"Ubf{s}", [128, DH]); a.sb(f"Ks{s}", [128, DH])
            a.sb(f"Sa{s}", [64, DH]); a.sb(f"Sb{s}", [64, DH])
            a.sb(f"nli{s}", [128, 1]); a.sb(f"bkr{s}", [128, 1]); a.sb(f"lC{s}", [64, 1])
            a.sb(f"gv{s}", [128, 1])
        a.ps("pTR", [64, 128]); a.ps("pA", [128, 128]); a.ps("pW", [128, DH])
        a.ps("pS", [64, DH])
        a.ps("pF0", [128, 192]); a.ps("pF1", [128, 192])
        a.ps("pM0", [128, 128]); a.ps("pM1", [128, 128])
        block = es.enter_context(nc.Block())
        sm = Sems(nc, ["sin", "sv", "ss", "stn", "sout"])
        sm.clear_then_barrier(block)

        @block.sync
        def _(sync):
            for dst, src_ in [(a.iotact, iotacl_d), (a.strict, strict_d), (a.identt, ident_d)]:
                sm.inc(sync.dma_start(out=dst[:], in_=src_[:, :]), "sin", 16)
            sm.inc(sync.dma_start(out=a.ioi[:], in_=ioi_d[:, :]), "sin", 16)
            sm.inc(sync.dma_start(out=a.ior[:], in_=ior_d[:, :]), "sin", 16)
            sm.inc(sync.dma_start(out=a.c128[:], in_=c128_d[:, :]), "sin", 16)
            sm.inc(sync.dma_start(out=a.ones1[:], in_=ones1_d[:, :]), "sin", 16)
            sm.inc(sync.dma_start(out=a.vals[:, 0:4], in_=lam4[None, :]), "sin", 16)
            sm.inc(sync.dma_start(out=a.tmp4[:], in_=eta4[None, :]), "sin", 16)
            sm.inc(sync.dma_start(out=a.kv0.ap().rearrange("p (s d) -> p s d", s=4),
                                  in_=k4[0]), "sin", 16)
            sm.inc(sync.dma_start(out=a.vv0.ap().rearrange("p (s d) -> p s d", s=4),
                                  in_=v4[0]), "sin", 16)

        @block.scalar
        def _(scalar):
            sm.wait(scalar, "sin")
            scalar.activation(a.vals[:, 0:4], a.vals[:, 0:4], AF.Sigmoid)
            scalar.drain()
            scalar.activation(a.tmp4[:], a.tmp4[:], AF.Tanh)
            scalar.drain()
            sm.inc(scalar.activation(a.vals[:, 4:8], a.vals[:, 0:4], AF.Ln), "ss")

        @block.vector
        def _(vector):
            sm.wait(vector, "ss")
            vector.tensor_scalar(a.vals[:, 8:12], a.vals[:, 0:4], -1.0, 1.0, ALU.mult, ALU.add)
            vector.drain()
            vector.tensor_mul(a.vals[:, 8:12], a.vals[:, 8:12], a.tmp4[:])
            vector.drain()
            vector.tensor_scalar_mul(a.vals[:, 12:16], a.vals[:, 8:12], 1.0 / DH)
            vector.drain()
            sm.inc(vector.tensor_scalar_mul(a.vals[:, 8:12], a.vals[:, 8:12], -1.0 / DH), "sv")

        @block.tensor
        def _(tensor):
            sm.wait(tensor, "sv")
            sm.wait(tensor, "sin")
            sm.inc(tensor.matmul(a.pA[:, 0:16], a.ones1[:], a.vals[:], start=True, stop=True), "stn")

        @block.scalar
        def _(scalar):
            sm.wait(scalar, "stn")
            sm.inc(scalar.copy(a.bct[:], a.pA[:, 0:16]), "ss")

        for s in range(4):
            lnl = a.bct[:, 4 + s:5 + s]

            @block.scalar
            def _(scalar, s=s, lnl=lnl):
                sm.wait(scalar, "ss")
                scalar.activation(getattr(a, f"mskL{s}")[:], a.iotact[:], AF.Exp, scale=lnl)
                scalar.activation(getattr(a, f"nli{s}")[:], a.ioi[:], AF.Exp, scale=lnl)
                scalar.activation(getattr(a, f"bkr{s}")[:], a.ior[:], AF.Exp, scale=lnl)
                sm.inc(scalar.activation(getattr(a, f"lC{s}")[:], a.c128[:], AF.Exp,
                                         scale=lnl[0:64, :]), "ss")
                scalar.drain()
                scalar.mul(getattr(a, f"nli{s}")[:], getattr(a, f"nli{s}")[:], -1.0)
                sm.inc(scalar.mul(getattr(a, f"bkr{s}")[:], getattr(a, f"bkr{s}")[:],
                                  a.bct[:, 8 + s:9 + s]), "ss")

            @block.vector
            def _(vector, s=s):
                sm.wait(vector, "ss")
                vector.tensor_tensor(getattr(a, f"mskL{s}")[:], getattr(a, f"mskL{s}")[:],
                                     a.strict[:], ALU.mult)
                vector.drain()
                vector.tensor_scalar(getattr(a, f"mskL{s}")[:], getattr(a, f"mskL{s}")[:],
                                     a.bct[:, 12 + s:13 + s], None, ALU.mult)
                vector.memset(getattr(a, f"Sa{s}")[:], 0.0)
                sm.inc(vector.memset(getattr(a, f"Sb{s}")[:], 0.0), "sv")

            @block.tensor
            def _(tensor, s=s):
                sm.wait(tensor, "sv")
                sm.inc(tensor.transpose(a.pM0[:], getattr(a, f"mskL{s}")[:], a.identt[:]), "stn")

            @block.scalar
            def _(scalar, s=s):
                sm.wait(scalar, "stn")
                sm.inc(scalar.copy(getattr(a, f"mskU{s}")[:], a.pM0[:]), "ss")

        rd = {"pTR": 0, "pA": 0, "pW": 0, "pS": 0,
              "pF": [(0, 0), (0, 0)], "pM": [0, 0]}
        load_done = [0] * (nch + 1)
        load_done[0] = sm.c["sin"]
        sdone = [0] * 4
        sdone_mm = [0] * 4
        last = {"kv_read": [0, 0], "kvv_read": [0, 0], "kvs_read": [0, 0], "ot_st": [0, 0]}

        for c in range(nch):
            buf = c % 2
            kv = getattr(a, f"kv{buf}"); vv = getattr(a, f"vv{buf}"); ot = getattr(a, f"ot{buf}")

            if c + 1 < nch:
                nbuf = (c + 1) % 2

                @block.sync
                def _(sync, c=c, nbuf=nbuf):
                    sm.wait(sync, "stn", last["kv_read"][nbuf])
                    sm.wait(sync, "sv", last["kvv_read"][nbuf])
                    sm.wait(sync, "ss", last["kvs_read"][nbuf])
                    sm.inc(sync.dma_start(out=getattr(a, f"kv{nbuf}").ap().rearrange("p (s d) -> p s d", s=4),
                                          in_=k4[c + 1]), "sin", 16)
                    sm.inc(sync.dma_start(out=getattr(a, f"vv{nbuf}").ap().rearrange("p (s d) -> p s d", s=4),
                                          in_=v4[c + 1]), "sin", 16)

                load_done[c + 1] = sm.c["sin"]

            ST = []
            for s in range(4):
                ST.append(dict(
                    P=s % 2,
                    pF=(a.pF0, a.pF1)[s % 2], pM=(a.pM0, a.pM1)[s % 2],
                    mskL=getattr(a, f"mskL{s}"), mskU=getattr(a, f"mskU{s}"),
                    MU=[getattr(a, f"MUa{s}"), getattr(a, f"MUb{s}")],
                    Mt=[getattr(a, f"Mta{s}"), getattr(a, f"Mtb{s}")],
                    kdT=getattr(a, f"kdT{s}"), Ubf=getattr(a, f"Ubf{s}"),
                    Ks=getattr(a, f"Ks{s}"), gv=getattr(a, f"gv{s}"),
                    Scur=getattr(a, f"Sa{s}") if c % 2 == 0 else getattr(a, f"Sb{s}"),
                    Snxt=getattr(a, f"Sb{s}") if c % 2 == 0 else getattr(a, f"Sa{s}"),
                    kap=kv[:, s * DH:(s + 1) * DH], vap=vv[:, s * DH:(s + 1) * DH],
                ))

            for s in range(4):
                st = ST[s]

                @block.vector
                def _(vector, c=c, s=s, st=st):
                    if s == 0:
                        sm.wait(vector, "sin", load_done[c])
                    vector.tensor_mul(a.kk[:], st["kap"], st["kap"])
                    vector.drain()
                    vector.reduce_sum(a.av[:], a.kk[:], axis=AX)
                    vector.drain()
                    sm.inc(vector.tensor_scalar(st["gv"][:], a.av[:], a.bct[:, 12 + s:13 + s], 1.0,
                                                ALU.mult, ALU.add), "sv")

                st["v_g"] = sm.c["sv"]

                @block.scalar
                def _(scalar, c=c, s=s, st=st):
                    if s == 0:
                        sm.wait(scalar, "sin", load_done[c])
                    sm.wait(scalar, "stn", sdone_mm[s])
                    sm.inc(scalar.mul(st["Ks"][:], st["kap"], getattr(a, f"bkr{s}")[:]), "ss")

                st["s_ks"] = sm.c["ss"]

            for s in range(4):
                st = ST[s]

                @block.tensor
                def _(tensor, c=c, s=s, st=st):
                    if s == 0:
                        sm.wait(tensor, "sin", load_done[c])
                    sm.wait(tensor, "sv", rd["pTR"])
                    sm.inc(tensor.transpose(a.pTR[:], st["kap"], a.identt[:]), "stn")

                t_tr = sm.c["stn"]

                @block.vector
                def _(vector, t_tr=t_tr, st=st):
                    sm.wait(vector, "stn", t_tr)
                    sm.inc(vector.tensor_copy(st["kdT"][:], a.pTR[:]), "sv")

                st["s_kdT"] = sm.c["sv"]
                rd["pTR"] = st["s_kdT"]

            for s in range(4):
                st = ST[s]

                @block.tensor
                def _(tensor, c=c, s=s, st=st):
                    sm.wait(tensor, "sv", max(st["s_kdT"], rd["pA"], rd["pW"], sdone[s]))
                    sm.inc(tensor.matmul(a.pA[:], st["kdT"][:], st["kdT"][:], start=True, stop=True), "stn")
                    sm.inc(tensor.matmul(a.pW[:], st["kdT"][:], st["Scur"][:], start=True, stop=True), "stn")

                t_AW = sm.c["stn"]

                @block.vector
                def _(vector, c=c, s=s, st=st, t_AW=t_AW):
                    sm.wait(vector, "stn", t_AW)
                    vector.tensor_tensor(st["MU"][0][:, 0:128], a.pA[:], st["mskL"][:], ALU.mult)
                    vector.tensor_tensor(st["Mt"][0][:], a.pA[:], st["mskU"][:], ALU.mult)
                    sm.inc(vector.scalar_tensor_tensor(
                        st["MU"][0][:, 128:192], a.pW[:], getattr(a, f"nli{s}")[:],
                        st["vap"], ALU.mult, ALU.add), "sv")

                st["v_mu"] = sm.c["sv"]
                rd["pA"] = sm.c["sv"]
                rd["pW"] = sm.c["sv"]
                st["s_m"] = 0
                st["s_mt"] = 0

            for lev in range(NLEV - 1):
                cur = lev % 2
                nxt = 1 - cur
                for s in range(4):
                    st = ST[s]
                    P = st["P"]
                    pF = st["pF"]; pM = st["pM"]; MU = st["MU"]; Mt = st["Mt"]

                    @block.tensor
                    def _(tensor, st=st, P=P, pF=pF, pM=pM, MU=MU, Mt=Mt, cur=cur):
                        fs, fv = rd["pF"][P]
                        sm.wait(tensor, "sv", max(st["v_mu"], fv, st["s_m"], st["s_mt"],
                                                  rd["pM"][P], fs))
                        sm.inc(tensor.matmul(pF[:], Mt[cur][:], MU[cur][:, 0:192],
                                             start=True, stop=True), "stn")
                        sm.inc(tensor.matmul(pM[:], MU[cur][:, 0:128], Mt[cur][:],
                                             start=True, stop=True), "stn")

                    t_m2 = sm.c["stn"]
                    t_f = t_m2 - 1

                    @block.vector
                    def _(vector, cur=cur, nxt=nxt, t_f=t_f, pF=pF, MU=MU):
                        sm.wait(vector, "stn", t_f)
                        vector.tensor_copy(MU[nxt][:, 0:128], pF[:, 0:128])
                        sm.inc(vector.tensor_add(MU[nxt][:, 128:192], MU[cur][:, 128:192],
                                                 pF[:, 128:192]), "sv")

                    st["s_m"] = sm.c["sv"]
                    st["v_mu"] = sm.c["sv"]

                    @block.vector
                    def _(vector, nxt=nxt, t_m2=t_m2, pM=pM, Mt=Mt):
                        sm.wait(vector, "stn", t_m2)
                        sm.inc(vector.tensor_copy(Mt[nxt][:], pM[:]), "sv")

                    st["s_mt"] = sm.c["sv"]
                    rd["pF"][P] = (0, st["v_mu"])
                    rd["pM"][P] = st["s_mt"]

            for s in range(4):
                st = ST[s]
                cur = (NLEV - 1) % 2
                MU = st["MU"]; Mt = st["Mt"]

                @block.tensor
                def _(tensor, st=st, cur=cur, MU=MU, Mt=Mt):
                    sm.wait(tensor, "sv", max(st["v_mu"], rd["pW"], st["s_m"], st["s_mt"]))
                    sm.inc(tensor.matmul(a.pW[:], Mt[cur][:], MU[cur][:, 128:192],
                                         start=True, stop=True), "stn")

                t_fin = sm.c["stn"]

                @block.vector
                def _(vector, st=st, cur=cur, t_fin=t_fin, MU=MU, c=c, s=s, buf=buf, ot=ot):
                    sm.wait(vector, "stn", t_fin)
                    vector.tensor_add(st["Ubf"][:], MU[cur][:, 128:192], a.pW[:])
                    if c >= 2:
                        sm.wait(vector, "sout", last["ot_st"][buf])
                    vector.drain()
                    sm.inc(vector.scalar_tensor_tensor(
                        ot[:, s * DH:(s + 1) * DH], st["Ubf"][:], st["gv"][:], st["vap"],
                        ALU.mult, ALU.subtract), "sv")

                st["v_ubf"] = sm.c["sv"]
                rd["pW"] = sm.c["sv"]

                @block.tensor
                def _(tensor, st=st):
                    sm.wait(tensor, "ss", st["s_ks"])
                    sm.wait(tensor, "sv", max(st["v_ubf"], rd["pS"]))
                    sm.inc(tensor.matmul(a.pS[:], st["Ks"][:], st["Ubf"][:], start=True, stop=True), "stn")

                t_s = sm.c["stn"]
                sdone_mm[s] = t_s

                @block.vector
                def _(vector, t_s=t_s, s=s, st=st):
                    sm.wait(vector, "stn", t_s)
                    sm.inc(vector.scalar_tensor_tensor(
                        st["Snxt"][:], st["Scur"][:], getattr(a, f"lC{s}")[:], a.pS[:],
                        ALU.mult, ALU.add), "sv")

                rd["pS"] = sm.c["sv"]
                sdone[s] = sm.c["sv"]

            v_all_o = sm.c["sv"]

            @block.gpsimd
            def _(gpsimd, c=c, buf=buf, v_all_o=v_all_o, ot=ot):
                sm.wait(gpsimd, "sv", v_all_o)
                sm.inc(gpsimd.dma_start(out=o4[c], in_=ot.ap().rearrange("p (s d) -> p s d", s=4)),
                       "sout", 16)

            last["ot_st"][buf] = sm.c["sout"]
            last["kv_read"][buf] = sm.c["stn"]
            last["kvv_read"][buf] = sm.c["sv"]
            last["kvs_read"][buf] = sm.c["ss"]

        @block.gpsimd
        def _(gpsimd):
            sm.wait(gpsimd, "sout")
    return nc


def build_l3():
    nc = bass.Bass("TRN2", target_bir_lowering=False, detect_race_conditions=False)
    x = nc.dram_tensor("x", [TC, D], F32, kind="ExternalInput").ap()
    st = nc.dram_tensor("st", [TC, D], F32, kind="ExternalInput").ap()
    od = nc.dram_tensor("od", [TC, D], F32, kind="ExternalInput").ap()
    ow = nc.dram_tensor("ow", [D], F32, kind="ExternalInput").ap()
    ob = nc.dram_tensor("ob", [D], F32, kind="ExternalInput").ap()
    wr = nc.dram_tensor("wr", [D, E], F32, kind="ExternalInput").ap()
    resid = nc.dram_tensor("resid", [TC, D], F32, kind="ExternalOutput").ap()
    xt2o = nc.dram_tensor("xt2o", [TC, D], F32, kind="ExternalOutput").ap()
    lgo = nc.dram_tensor("lgo", [TC, E], F32, kind="ExternalOutput").ap()
    ident_d = nc.inline_tensor(np.eye(128, dtype=np.float32), "identc_d").ap()

    NT = TC // 128
    with contextlib.ExitStack() as es:
        a = Alloc(nc, es)
        for nm, sh in [("xt", [128, D]), ("stt", [128, D]), ("odt", [128, D]),
                       ("z", [128, D]), ("lnb", [128, D]), ("xt2", [128, D]),
                       ("rez", [128, D]), ("xt2T", [128, D]), ("lgt", [128, E]),
                       ("ident", [128, 128]), ("owt", [128, D]), ("obt", [128, D]),
                       ("wrt", [128, 8 * E]), ("mu", [128, 1]), ("m2", [128, 1]),
                       ("var", [128, 1]), ("rstd", [128, 1]), ("mrs", [128, 1]),
                       ("sq", [128, D])]:
            a.sb(nm, sh)
        a.ps("pT", [128, 128]); a.ps("pL", [128, E])
        block = es.enter_context(nc.Block())
        sm = Sems(nc, ["sin", "sv", "ss", "stn", "sout"])
        sm.clear_then_barrier(block)

        @block.sync
        def _(sync):
            sm.inc(sync.dma_start(out=a.ident[:], in_=ident_d[:, :]), "sin", 16)
            sm.inc(sync.dma_start(out=a.owt[:], in_=ow[None, :].to_broadcast([128, D])), "sin", 16)
            sm.inc(sync.dma_start(out=a.obt[:], in_=ob[None, :].to_broadcast([128, D])), "sin", 16)
            sm.inc(sync.dma_start(out=a.wrt.ap().rearrange("p (c e) -> p c e", c=8),
                                  in_=wr.rearrange("(c p) e -> p c e", p=128)), "sin", 16)

        for i in range(NT):
            @block.sync
            def _(sync, i=i):
                if i >= 1:
                    sm.wait(sync, "sv")
                    sm.wait(sync, "stn")
                sm.inc(sync.dma_start(out=a.xt[:], in_=x[i * 128:(i + 1) * 128, :]), "sin", 16)
                sm.inc(sync.dma_start(out=a.stt[:], in_=st[i * 128:(i + 1) * 128, :]), "sin", 16)
                sm.inc(sync.dma_start(out=a.odt[:], in_=od[i * 128:(i + 1) * 128, :]), "sin", 16)

            @block.vector
            def _(vector, i=i):
                sm.wait(vector, "sin")
                if i >= 1:
                    sm.wait(vector, "sout")
                sm.inc(vector.tensor_add(a.z[:], a.stt[:], a.odt[:]), "sv")

            @block.vector
            def _(vector):
                vector.drain()
            emit_ln(block.vector, block.scalar, sm, a.z[:], a.lnb[:], D,
                    a.mu, a.m2, a.var, a.rstd, a.mrs, a.sq, LN_EPS, "sv", "ss")

            @block.vector
            def _(vector):
                vector.drain()
                vector.tensor_mul(a.lnb[:], a.lnb[:], a.owt[:])
                vector.drain()
                vector.tensor_add(a.lnb[:], a.lnb[:], a.obt[:])
                vector.drain()
                sm.inc(vector.tensor_add(a.rez[:], a.xt[:], a.lnb[:]), "sv")

            @block.vector
            def _(vector):
                vector.drain()
            emit_ln(block.vector, block.scalar, sm, a.rez[:], a.xt2[:], D,
                    a.mu, a.m2, a.var, a.rstd, a.mrs, a.sq, LN_EPS, "sv", "ss")
            v_xt2 = sm.c["sv"]

            for c in range(8):
                @block.tensor
                def _(tensor, c=c):
                    sm.wait(tensor, "sv")
                    sm.wait(tensor, "ss")
                    sm.wait(tensor, "sin")
                    sm.inc(tensor.transpose(a.pT[:], a.xt2[:, c * 128:(c + 1) * 128], a.ident[:]), "stn")

                @block.scalar
                def _(scalar, c=c):
                    sm.wait(scalar, "stn")
                    sm.inc(scalar.copy(a.xt2T[:, c * 128:(c + 1) * 128], a.pT[:]), "ss")

            @block.tensor
            def _(tensor, i=i):
                sm.wait(tensor, "ss")
                if i >= 1:
                    sm.wait(tensor, "sv")
                for c in range(8):
                    ins = tensor.matmul(a.pL[:], a.xt2T[:, c * 128:(c + 1) * 128],
                                        a.wrt[:, c * E:(c + 1) * E],
                                        start=(c == 0), stop=(c == 7))
                sm.inc(ins, "stn")

            @block.vector
            def _(vector):
                sm.wait(vector, "stn")
                sm.inc(vector.tensor_copy(a.lgt[:], a.pL[:]), "sv")

            v_all = sm.c["sv"]

            @block.gpsimd
            def _(gpsimd, i=i, v_all=v_all, v_xt2=v_xt2):
                sm.wait(gpsimd, "sv", v_all)
                sm.inc(gpsimd.dma_start(out=resid[i * 128:(i + 1) * 128, :], in_=a.rez[:]), "sout", 16)
                sm.inc(gpsimd.dma_start(out=xt2o[i * 128:(i + 1) * 128, :], in_=a.xt2[:]), "sout", 16)
                sm.inc(gpsimd.dma_start(out=lgo[i * 128:(i + 1) * 128, :], in_=a.lgt[:]), "sout", 16)

        @block.gpsimd
        def _(gpsimd):
            sm.wait(gpsimd, "sout")
    return nc


def build_l6():
    nc = bass.Bass("TRN2", target_bir_lowering=False, detect_race_conditions=False)
    rz = nc.dram_tensor("rz", [TC, D], F32, kind="ExternalInput").ap()
    xt2 = nc.dram_tensor("xt2", [TC, D], F32, kind="ExternalInput").ap()
    yfull = nc.dram_tensor("yfull", [EC, D], F32, kind="ExternalInput").ap()
    slot = nc.dram_tensor("slot", [TC], F32, kind="ExternalInput").ap()
    gate = nc.dram_tensor("gate", [TC], F32, kind="ExternalInput").ap()
    out = nc.dram_tensor("out", [TC, D], F32, kind="ExternalOutput").ap()

    NT = TC // 128
    with contextlib.ExitStack() as es:
        a = Alloc(nc, es)
        for nm, sh in [("rzt", [128, D]), ("xtt", [128, D]), ("yt", [128, D]),
                       ("ot", [128, D]), ("gt", [128, 1]), ("sl", [128, 1])]:
            a.sb(nm, sh)
        a.sb("sli", [128, 1], I32)
        block = es.enter_context(nc.Block())
        sm = Sems(nc, ["sin", "sv", "sg", "sout"])
        sm.clear_then_barrier(block)

        for i in range(NT):
            @block.sync
            def _(sync, i=i):
                if i >= 1:
                    sm.wait(sync, "sv")
                sm.inc(sync.dma_start(out=a.rzt[:], in_=rz[i * 128:(i + 1) * 128, :]), "sin", 16)
                sm.inc(sync.dma_start(out=a.xtt[:], in_=xt2[i * 128:(i + 1) * 128, :]), "sin", 16)
                sm.inc(sync.dma_start(out=a.gt[:], in_=gate[i * 128:(i + 1) * 128, None]), "sin", 16)
                sm.inc(sync.dma_start(out=a.sl[:], in_=slot[i * 128:(i + 1) * 128, None]), "sin", 16)

            @block.vector
            def _(vector, i=i):
                sm.wait(vector, "sin")
                if i >= 1:
                    sm.wait(vector, "sg")
                sm.inc(vector.tensor_copy(a.sli[:], a.sl[:]), "sv")

            @block.gpsimd
            def _(gpsimd, i=i):
                sm.wait(gpsimd, "sv")
                sm.inc(gpsimd.indirect_dma_start(
                    a.yt[:], None, yfull[:, :],
                    IndirectOffsetOnAxis(a.sli[:], 0),
                    bounds_check=EC - 1, oob_is_err=False), "sg", 16)

            @block.vector
            def _(vector, i=i):
                sm.wait(vector, "sg")
                if i >= 1:
                    sm.wait(vector, "sout")
                vector.scalar_tensor_tensor(a.ot[:], a.yt[:], a.gt[:], a.rzt[:],
                                            ALU.mult, ALU.add)
                vector.drain()
                sm.inc(vector.tensor_add(a.ot[:], a.ot[:], a.xtt[:]), "sv")

            @block.gpsimd
            def _(gpsimd, i=i):
                sm.wait(gpsimd, "sv")
                sm.inc(gpsimd.dma_start(out=out[i * 128:(i + 1) * 128, :], in_=a.ot[:]), "sout", 16)

        @block.gpsimd
        def _(gpsimd):
            sm.wait(gpsimd, "sout")
    return nc


def build_l5():
    nc = bass.Bass("TRN2", target_bir_lowering=False, detect_race_conditions=False)
    lg_d = nc.dram_tensor("lg", [T, E], F32, kind="ExternalInput").ap()
    xt_d = nc.dram_tensor("xt", [T, D], F32, kind="ExternalInput").ap()
    wg_d = nc.dram_tensor("wg4", [4, D, D], F32, kind="ExternalInput").ap()
    wu_d = nc.dram_tensor("wu4", [4, D, D], F32, kind="ExternalInput").ap()
    wd_d = nc.dram_tensor("wd4", [4, D, D], F32, kind="ExternalInput").ap()
    soff_d = nc.dram_tensor("soff", [128, 1], F32, kind="ExternalInput").ap()
    y4_d = nc.dram_tensor("y4", [4 * CCAP, D], F32, kind="ExternalOutput").ap()
    slot_d = nc.dram_tensor("slot", [T], F32, kind="ExternalOutput").ap()
    gate_d = nc.dram_tensor("gate", [T], F32, kind="ExternalOutput").ap()
    inv_d = nc.dram_tensor("inv", [4 * CCAP, 1], I32, kind="Internal").ap()

    ident_d = nc.inline_tensor(np.eye(128, dtype=np.float32), "identc_d").ap()
    ones_d = nc.inline_tensor(np.ones((128, 128), dtype=np.float32), "ones_d").ap()
    ioe_d = nc.inline_tensor(np.tile(np.arange(E, dtype=np.float32), (128, 1)), "ioe_d").ap()
    ioe100_d = nc.inline_tensor(np.tile(np.arange(E, dtype=np.float32) - 100.0, (128, 1)), "ioe100_d").ap()
    tokc_np = (np.arange(128)[:, None] * 128 + np.arange(128)[None, :]).astype(np.int32)
    tokc_d = nc.inline_tensor(tokc_np, "tokc_d").ap()

    NF = T // 128 // 128
    with contextlib.ExitStack() as es:
        a = Alloc(nc, es)
        for p in range(4):
            a.sb(f"pool{p}", [128, 5120])
        for nm, sh in [("ident", [128, 128]), ("ones", [128, 128]), ("ioe", [128, E]),
                       ("ioe100", [128, E]), ("lmax", [128, 128]), ("ssum", [128, 128]),
                       ("gatee", [128, 128]), ("top1", [128, 128]), ("tot", [128, E]),
                       ("totT", [32, 128]), ("cum", [32, 128]), ("excl", [32, 128]),
                       ("off", [128, E]), ("posi", [128, 128]), ("acc", [128, 128]),
                       ("slotf", [128, 128]), ("slotl", [128, 128]), ("sofft", [128, 1]), ("acc2", [128, 128]),
                       ("wbuf", [128, 8 * D]), ("btile", [128, D]), ("ytile", [128, D])]:
            a.sb(nm, sh)
        a.sb("tokc", [128, 128], I32)
        a.sb("pad", [128, 20], I32)
        a.sb("sli", [128, 128], I32)
        a.sb("gidx", [128, 1], I32)
        a.ps("pTs", [128, 128]); a.ps("pTr", [32, 128]); a.ps("pTr2", [128, 32])
        a.ps("pm0", [128, 320]); a.ps("pm1", [128, 320])
        a.ps("pm2", [128, 320]); a.ps("pm3", [128, 320])
        block = es.enter_context(nc.Block())
        sm = Sems(nc, ["sin", "sv", "ss", "stn", "sg", "sout"])
        sm.clear_then_barrier(block)

        lgv = a.pool0.ap()[:, 0:4096].rearrange("p (f e) -> p f e", e=E)
        ohv = a.pool1.ap()[:, 0:4096].rearrange("p (e f) -> p e f", e=E)
        cntv = a.pool2.ap()[:, 0:4096].rearrange("p (e f) -> p e f", e=E)
        t1v = a.pool3.ap()[:, 0:4096].rearrange("p (e f) -> p e f", e=E)

        @block.sync
        def _(sync):
            for dst, src in [(a.ident, ident_d), (a.ones, ones_d), (a.ioe, ioe_d),
                             (a.ioe100, ioe100_d), (a.tokc, tokc_d), (a.sofft, soff_d)]:
                sm.inc(sync.dma_start(out=dst[:], in_=src[:, :]), "sin", 16)
            sm.inc(sync.dma_start(out=lgv, in_=lg_d.rearrange("(p f) e -> p f e", p=128)), "sin", 16)

        @block.vector
        def _(vector):
            sm.wait(vector, "sin")
            vector.reduce_max(a.lmax[:], lgv, axis=AX)
            vector.drain()
            sm.inc(vector.tensor_tensor(
                a.pool1.ap()[:, 0:4096].rearrange("p (f e) -> p f e", e=E), lgv,
                a.lmax.ap().unsqueeze(2).to_broadcast([128, 128, E]), ALU.subtract), "sv")

        @block.scalar
        def _(scalar):
            sm.wait(scalar, "sv")
            sm.inc(scalar.activation(a.pool1[:, 0:4096], a.pool1[:, 0:4096], AF.Exp), "ss")

        @block.vector
        def _(vector):
            sm.wait(vector, "ss")
            vector.reduce_sum(a.ssum.ap().unsqueeze(2),
                              a.pool1.ap()[:, 0:4096].rearrange("p (f e) -> p f e", e=E), axis=AX)
            vector.drain()
            vector.reciprocal(a.gatee[:], a.ssum[:])
            vector.tensor_tensor(
                a.pool1.ap()[:, 0:4096].rearrange("p (f e) -> p f e", e=E), lgv,
                a.lmax.ap().unsqueeze(2).to_broadcast([128, 128, E]), ALU.is_ge)
            vector.drain()
            vector.tensor_tensor(
                a.pool1.ap()[:, 0:4096].rearrange("p (f e) -> p f e", e=E),
                a.pool1.ap()[:, 0:4096].rearrange("p (f e) -> p f e", e=E),
                a.ioe100.ap().unsqueeze(1).to_broadcast([128, 128, E]), ALU.mult)
            vector.drain()
            vector.tensor_reduce(a.top1.ap().unsqueeze(2),
                                 a.pool1.ap()[:, 0:4096].rearrange("p (f e) -> p f e", e=E),
                                 AX, ALU.min)
            vector.drain()
            vector.tensor_scalar_add(a.top1[:], a.top1[:], 100.0)
            vector.drain()
            sm.inc(vector.tensor_tensor(
                ohv, a.top1.ap().unsqueeze(1).to_broadcast([128, E, 128]),
                a.ioe.ap().unsqueeze(2).to_broadcast([128, E, 128]), ALU.is_equal), "sv")

        @block.vector
        def _(vector):
            for e in range(E):
                vector.tensor_tensor_scan(cntv[:, e, :], a.ones[:], ohv[:, e, :],
                                          0.0, ALU.mult, ALU.add)
            vector.drain()
            sm.inc(vector.tensor_copy(a.tot.ap().unsqueeze(2), cntv[:, :, 127:128]), "sv")

        @block.tensor
        def _(tensor):
            sm.wait(tensor, "sv")
            sm.wait(tensor, "sin")
            sm.inc(tensor.transpose(a.pTr[:], a.tot[:], a.ident[:]), "stn")

        @block.scalar
        def _(scalar):
            sm.wait(scalar, "stn")
            sm.inc(scalar.copy(a.totT[:], a.pTr[:]), "ss")

        @block.vector
        def _(vector):
            sm.wait(vector, "ss")
            vector.tensor_tensor_scan(a.cum[:], a.ones[0:32, :], a.totT[:], 0.0,
                                      ALU.mult, ALU.add)
            vector.drain()
            vector.memset(a.excl[:, 0:1], 0.0)
            vector.tensor_copy(a.excl[:, 1:128], a.cum[:, 0:127])
            sm.inc(vector.drain(), "sv")

        @block.tensor
        def _(tensor):
            sm.wait(tensor, "sv")
            sm.inc(tensor.transpose(a.pTr2[:], a.excl[:], a.ident[0:32, 0:32]), "stn")

        @block.scalar
        def _(scalar):
            sm.wait(scalar, "stn")
            sm.inc(scalar.copy(a.off[:], a.pTr2[:]), "ss")

        @block.vector
        def _(vector):
            sm.wait(vector, "ss")
            vector.tensor_tensor(t1v, cntv,
                                 a.off.ap().unsqueeze(2).to_broadcast([128, E, 128]), ALU.add)
            vector.drain()
            vector.tensor_tensor(t1v, t1v, ohv, ALU.mult)
            vector.drain()
            vector.tensor_reduce(a.posi.ap().unsqueeze(2),
                                 a.pool3.ap()[:, 0:4096].rearrange("p (e f) -> p e f", e=E).transpose([0, 2, 1]),
                                 AX, ALU.add)
            vector.drain()
            vector.tensor_scalar(a.acc[:], a.posi[:], 641.0, None, ALU.is_lt)
            vector.scalar_tensor_tensor(a.slotf[:], a.top1[:], 640.0, a.posi[:],
                                        ALU.mult, ALU.add)
            vector.drain()
            vector.tensor_scalar_add(a.slotf[:], a.slotf[:], -20481.0)
            vector.drain()
            vector.tensor_mul(a.slotf[:], a.slotf[:], a.acc[:])
            vector.drain()
            vector.tensor_scalar_add(a.slotf[:], a.slotf[:], 20480.0)
            vector.tensor_mul(a.gatee[:], a.gatee[:], a.acc[:])
            vector.drain()
            vector.scalar_tensor_tensor(a.slotl[:], a.sofft[:].to_broadcast([128, 128]),
                                        -1.0, a.slotf[:], ALU.mult, ALU.add)
            vector.drain()
            vector.tensor_scalar(a.acc2[:], a.slotl[:], 0.0, None, ALU.is_lt)
            vector.drain()
            vector.scalar_tensor_tensor(a.slotl[:], a.acc2[:], 30000.0, a.slotl[:],
                                        ALU.mult, ALU.add)
            vector.drain()
            vector.tensor_copy(a.sli[:], a.slotl[:])
            sm.inc(vector.drain(), "sv")

        v_route = sm.c["sv"]

        @block.gpsimd
        def _(gpsimd):
            sm.wait(gpsimd, "sv", v_route)
            sm.inc(gpsimd.dma_start(out=slot_d.rearrange("(p f) -> p f", p=128), in_=a.slotf[:]), "sout", 16)
            sm.inc(gpsimd.dma_start(out=gate_d.rearrange("(p f) -> p f", p=128), in_=a.gatee[:]), "sout", 16)
            gpsimd.memset(a.pad[:], 16384)
            sm.inc(gpsimd.dma_start(out=inv_d.rearrange("(p g) one -> p (g one)", p=128),
                                    in_=a.pad[:]), "sg", 16)
            sm.wait(gpsimd, "sg")
            bc = gpsimd.to_reg(4 * CCAP - 1)
            for j in range(128):
                sm.inc(gpsimd.indirect_dma_start(
                    inv_d[:, :], IndirectOffsetOnAxis(a.sli[:, j:j + 1], 0),
                    a.tokc[:, j:j + 1], None,
                    bounds_check=bc, oob_is_err=False), "sg", 16)
            sm.wait(gpsimd, "sg")

        g_scat = sm.c["sg"]
        bcg = [None]

        @block.gpsimd
        def _(gpsimd):
            bcg[0] = gpsimd.to_reg(T - 1)

        sv_next_wait = 0
        for e in range(4):
            @block.sync
            def _(sync, e=e):
                if e > 0:
                    sm.wait(sync, "stn")
                sm.inc(sync.dma_start(out=a.wbuf.ap().rearrange("p (hc f) -> p hc f", hc=8),
                                      in_=wg_d[e].rearrange("(hc p) f -> p hc f", p=128)), "sin", 16)

            w_g = sm.c["sin"]

            for t in range(5):
                @block.gpsimd
                def _(gpsimd, e=e, t=t):
                    sm.wait(gpsimd, "sg", g_scat)
                    if not (e == 0 and t == 0):
                        sm.wait(gpsimd, "stn")
                    sm.inc(gpsimd.dma_start(out=a.gidx[:], in_=inv_d[e * CCAP + t * 128:e * CCAP + (t + 1) * 128, :]), "sg", 16)
                    sm.wait(gpsimd, "sg")
                    sm.inc(gpsimd.indirect_dma_start(
                        a.btile[:], None, xt_d[:, :],
                        IndirectOffsetOnAxis(a.gidx[:], 0),
                        bounds_check=bcg[0], oob_is_err=False), "sg", 16)

                for hc in range(8):
                    @block.tensor
                    def _(tensor, t=t, hc=hc):
                        sm.wait(tensor, "sg")
                        sm.wait(tensor, "ss")
                        sm.inc(tensor.transpose(a.pTs[:], a.btile[:, hc * 128:(hc + 1) * 128],
                                                a.ident[:]), "stn")

                    @block.scalar
                    def _(scalar, t=t, hc=hc):
                        sm.wait(scalar, "stn")
                        sm.inc(scalar.copy(a.pool0[:, hc * 640 + t * 128: hc * 640 + (t + 1) * 128],
                                           a.pTs[:]), "ss")

            s_bufT = sm.c["ss"]

            for fc in range(8):
                for half, pm in ((0, a.pm0), (1, a.pm1)):
                    @block.tensor
                    def _(tensor, e=e, fc=fc, half=half, pm=pm, w_g=w_g):
                        sm.wait(tensor, "sin", w_g)
                        sm.wait(tensor, "ss")
                        sm.wait(tensor, "sv")
                        for hc in range(8):
                            ins = tensor.matmul(
                                pm[:], a.wbuf[:, hc * D + fc * 128: hc * D + fc * 128 + 128],
                                a.pool0[:, hc * 640 + half * 320: hc * 640 + half * 320 + 320],
                                start=(hc == 0), stop=(hc == 7))
                        sm.inc(ins, "stn")

                    @block.scalar
                    def _(scalar, fc=fc, half=half, pm=pm):
                        sm.wait(scalar, "stn")
                        sm.inc(scalar.activation(
                            a.pool1[:, fc * 640 + half * 320: fc * 640 + half * 320 + 320],
                            pm[:], AF.Silu), "ss")

            @block.sync
            def _(sync, e=e):
                sm.wait(sync, "stn")
                sm.inc(sync.dma_start(out=a.wbuf.ap().rearrange("p (hc f) -> p hc f", hc=8),
                                      in_=wu_d[e].rearrange("(hc p) f -> p hc f", p=128)), "sin", 16)

            w_u = sm.c["sin"]
            for fc in range(8):
                for half, pm in ((0, a.pm2), (1, a.pm3)):
                    @block.tensor
                    def _(tensor, e=e, fc=fc, half=half, pm=pm, w_u=w_u):
                        sm.wait(tensor, "sin", w_u)
                        sm.wait(tensor, "sv")
                        for hc in range(8):
                            ins = tensor.matmul(
                                pm[:], a.wbuf[:, hc * D + fc * 128: hc * D + fc * 128 + 128],
                                a.pool0[:, hc * 640 + half * 320: hc * 640 + half * 320 + 320],
                                start=(hc == 0), stop=(hc == 7))
                        sm.inc(ins, "stn")

                    @block.vector
                    def _(vector, fc=fc, half=half, pm=pm):
                        sm.wait(vector, "stn")
                        sm.wait(vector, "ss")
                        sl = slice(fc * 640 + half * 320, fc * 640 + half * 320 + 320)
                        sm.inc(vector.tensor_tensor(a.pool2[:, sl], a.pool1[:, sl], pm[:],
                                                    ALU.mult), "sv")

            @block.sync
            def _(sync, e=e):
                sm.wait(sync, "stn")
                sm.inc(sync.dma_start(out=a.wbuf.ap().rearrange("p (hc f) -> p hc f", hc=8),
                                      in_=wd_d[e].rearrange("(fc p) h -> p fc h", p=128)), "sin", 16)

            w_d = sm.c["sin"]
            for hc in range(8):
                for half, pm in ((0, a.pm0), (1, a.pm1)):
                    @block.tensor
                    def _(tensor, e=e, hc=hc, half=half, pm=pm, w_d=w_d):
                        sm.wait(tensor, "sin", w_d)
                        sm.wait(tensor, "sv")
                        sm.wait(tensor, "ss")
                        for fc in range(8):
                            ins = tensor.matmul(
                                pm[:], a.wbuf[:, fc * D + hc * 128: fc * D + hc * 128 + 128],
                                a.pool2[:, fc * 640 + half * 320: fc * 640 + half * 320 + 320],
                                start=(fc == 0), stop=(fc == 7))
                        sm.inc(ins, "stn")

                    @block.scalar
                    def _(scalar, hc=hc, half=half, pm=pm):
                        sm.wait(scalar, "stn")
                        sm.inc(scalar.copy(
                            a.pool3[:, hc * 640 + half * 320: hc * 640 + half * 320 + 320],
                            pm[:]), "ss")

            s_yT = sm.c["ss"]

            for t in range(5):
                for hc in range(8):
                    @block.tensor
                    def _(tensor, t=t, hc=hc, s_yT=s_yT):
                        sm.wait(tensor, "ss", s_yT)
                        sm.wait(tensor, "ss")
                        if t + hc == 0:
                            sm.wait(tensor, "sout")
                        sm.inc(tensor.transpose(a.pTs[:], a.pool3[:, hc * 640 + t * 128: hc * 640 + (t + 1) * 128],
                                                a.ident[:]), "stn")

                    @block.scalar
                    def _(scalar, t=t, hc=hc):
                        sm.wait(scalar, "stn")
                        if hc == 0:
                            sm.wait(scalar, "sout")
                        sm.inc(scalar.copy(a.ytile[:, hc * 128:(hc + 1) * 128], a.pTs[:]), "ss")

                @block.gpsimd
                def _(gpsimd, e=e, t=t):
                    sm.wait(gpsimd, "ss")
                    sm.inc(gpsimd.dma_start(
                        out=y4_d[e * CCAP + t * 128: e * CCAP + (t + 1) * 128, :],
                        in_=a.ytile[:]), "sout", 16)

        @block.gpsimd
        def _(gpsimd):
            sm.wait(gpsimd, "sout")
    return nc


def kernel(**inputs):
    cores = list(range(NCORE))
    x = np.ascontiguousarray(inputs["L0_attn_x"].reshape(T, D))
    WT = np.ascontiguousarray(inputs["L0_attn_W"].T)
    knw = np.ascontiguousarray(np.tile(inputs["L0_attn_k_norm_w"], H))

    nc1 = build_l1()
    maps = [{"x": x[c * TC:(c + 1) * TC], "WT": WT, "knw": knw} for c in cores]
    r1 = run_bass_kernel_spmd(nc1, maps, core_ids=cores)
    static = np.concatenate([r["static"] for r in r1.results], axis=0)
    kmat = np.concatenate([r["kout"] for r in r1.results], axis=0)

    k_str = kmat.reshape(2, 8192, H, DH).transpose(0, 2, 1, 3).reshape(2 * H, 8192, DH)
    v_str = static.reshape(2, 8192, H, DH).transpose(0, 2, 1, 3).reshape(2 * H, 8192, DH)
    lam_l = inputs["L0_attn_lam_logit"]
    eta_r = inputs["L0_attn_eta_raw"]
    nc2 = build_l2(nch=64)
    maps = []
    for c in cores:
        sl = slice(c * 4, (c + 1) * 4)
        heads = [(c * 4 + i) % H for i in range(4)]
        maps.append({
            "k4": np.ascontiguousarray(k_str[sl].reshape(4, 64, 128, DH).transpose(1, 2, 0, 3)),
            "v4": np.ascontiguousarray(v_str[sl].reshape(4, 64, 128, DH).transpose(1, 2, 0, 3)),
            "lam4": np.ascontiguousarray(lam_l[heads]),
            "eta4": np.ascontiguousarray(eta_r[heads]),
        })
    r2 = run_bass_kernel_spmd(nc2, maps, core_ids=cores)
    o_str = np.stack([r["o4"].transpose(2, 0, 1, 3).reshape(4, 8192, DH)
                      for r in r2.results]).reshape(2 * H, 8192, DH)
    o_dyn = np.ascontiguousarray(
        o_str.reshape(2, H, 8192, DH).transpose(0, 2, 1, 3).reshape(T, D))

    nc3 = build_l3()
    maps = [{"x": x[c * TC:(c + 1) * TC], "st": static[c * TC:(c + 1) * TC],
             "od": o_dyn[c * TC:(c + 1) * TC], "ow": inputs["L0_attn_o_norm_w"],
             "ob": inputs["L0_attn_o_norm_b"], "wr": inputs["L0_ffn_w_router"]}
            for c in cores]
    r3 = run_bass_kernel_spmd(nc3, maps, core_ids=cores)
    resid1 = np.concatenate([r["resid"] for r in r3.results], axis=0)
    xt2 = np.concatenate([r["xt2o"] for r in r3.results], axis=0)
    logits = np.concatenate([r["lgo"] for r in r3.results], axis=0)

    nc5 = build_l5()
    wg, wu, wd = inputs["L0_ffn_w_gate"], inputs["L0_ffn_w_up"], inputs["L0_ffn_w_down"]
    maps = [{"lg": logits, "xt": xt2,
             "wg4": np.ascontiguousarray(wg[c * 4:(c + 1) * 4]),
             "wu4": np.ascontiguousarray(wu[c * 4:(c + 1) * 4]),
             "wd4": np.ascontiguousarray(wd[c * 4:(c + 1) * 4]),
             "soff": np.full((128, 1), c * 4 * CCAP, dtype=np.float32)}
            for c in cores]
    r5 = run_bass_kernel_spmd(nc5, maps, core_ids=cores)
    yfull = np.concatenate([r["y4"] for r in r5.results], axis=0)
    slot = r5.results[0]["slot"]
    gate = r5.results[0]["gate"]

    nc6 = build_l6()
    maps = [{"rz": resid1[c * TC:(c + 1) * TC], "xt2": xt2[c * TC:(c + 1) * TC],
             "yfull": yfull, "slot": slot[c * TC:(c + 1) * TC],
             "gate": gate[c * TC:(c + 1) * TC]} for c in cores]
    r6 = run_bass_kernel_spmd(nc6, maps, core_ids=cores)
    out = np.concatenate([r["out"] for r in r6.results], axis=0)
    return out.reshape(2, 8192, D)


# version 03
# speedup vs baseline: 2.1840x; 1.1789x over previous
import contextlib
import numpy as np
import concourse.bass as bass
import concourse.mybir as mybir
from concourse.bass_utils import run_bass_kernel_spmd
from concourse.bass_core_rust import IndirectOffsetOnAxis


class Alloc:

    def __init__(self, nc, es):
        self.nc, self.es = nc, es

    def sb(self, name, shape, dtype=mybir.dt.float32):
        t = self.es.enter_context(self.nc.sbuf_tensor(name, shape, dtype))
        setattr(self, name, t)
        return t

    def ps(self, name, shape, dtype=mybir.dt.float32):
        t = self.es.enter_context(self.nc.psum_tensor(name, shape, dtype))
        setattr(self, name, t)
        return t

F32 = mybir.dt.float32
I32 = mybir.dt.int32
AF = mybir.ActivationFunctionType
ALU = mybir.AluOpType
AX = mybir.AxisListType.X

NCORE = 8
D = 1024
T = 16384
TC = T // NCORE
H = 16
DH = 64
CCAP = 640
E = 32
EC = E * CCAP
LN_EPS = 1e-5
RMS_EPS = float(np.finfo(np.float64).eps)


class Sems:

    def __init__(self, nc, names):
        self.nc = nc
        self.h = {}
        self.c = {}
        self.ctx = []
        for n in names:
            hh = nc.alloc_semaphore(n)
            self.h[n] = hh
            self.c[n] = 0

    def inc(self, ins, name, n=1):
        self.c[name] += n
        return ins.then_inc(self.h[name], n)

    def wait(self, eng, name, val=None):
        eng.wait_ge(self.h[name], self.c[name] if val is None else val)

    def clear_then_barrier(self, block):
        nc = self.nc
        ksr = nc._kernel_sem_range
        n_extra = 2 + (1 if nc._bir_kernel_barrier_sem is not None else 0) + len(nc._monotonic_sems)
        rng = range(ksr.start + n_extra, ksr.stop)
        nc.gpsimd.dma_reset(rng)
        nc.gpsimd.sem_clear(rng)
        nc.all_engine_barrier()


def emit_ln(v, sc, sm, xt, out, width, mu, m2, var, rstd, mrs, sq, eps, vsem, ssem):
    @v
    def _(vector):
        vector.reduce_sum(mu[:], xt, axis=AX)
        vector.tensor_mul(sq[:], xt, xt)
        vector.reduce_sum(m2[:], sq[:], axis=AX)
        vector.drain()
        vector.tensor_scalar_mul(mu[:], mu[:], 1.0 / width)
        vector.tensor_scalar_mul(m2[:], m2[:], 1.0 / width)
        vector.drain()
        vector.tensor_tensor(var[:], mu[:], mu[:], ALU.mult)
        vector.drain()
        vector.tensor_sub(var[:], m2[:], var[:])
        vector.drain()
        vector.tensor_scalar_add(var[:], var[:], eps)
        vector.drain()
        sm.inc(vector.reciprocal(var[:], var[:]), vsem)

    @sc
    def _(scalar):
        sm.wait(scalar, vsem)
        sm.inc(scalar.activation(rstd[:], var[:], AF.Sqrt), ssem)

    @v
    def _(vector):
        sm.wait(vector, ssem)
        vector.tensor_tensor(mrs[:], mu[:], rstd[:], ALU.mult)
        vector.drain()
        sm.inc(vector.scalar_tensor_tensor(
            out, xt, rstd[:], mrs[:].to_broadcast(out.shape),
            ALU.mult, ALU.subtract), vsem)


def build_l1():
    nc = bass.Bass("TRN2", target_bir_lowering=False, detect_race_conditions=False)
    x = nc.dram_tensor("x", [TC, D], F32, kind="ExternalInput").ap()
    WT = nc.dram_tensor("WT", [D, D], F32, kind="ExternalInput").ap()
    knw = nc.dram_tensor("knw", [D], F32, kind="ExternalInput").ap()
    static = nc.dram_tensor("static", [TC, D], F32, kind="ExternalOutput").ap()
    kout = nc.dram_tensor("kout", [TC, D], F32, kind="ExternalOutput").ap()
    ident_d = nc.inline_tensor(np.eye(128, dtype=np.float32), "identc_d").ap()

    NT = TC // 128
    with contextlib.ExitStack() as es:
        a = Alloc(nc, es)
        for nm, sh in [("xt", [128, D]), ("xln", [128, D]), ("xlnT", [128, D]),
                       ("wtile", [128, 8 * D]), ("ident", [128, 128]), ("knwt", [128, D]),
                       ("stat", [128, D]), ("ktile", [128, D]), ("mu", [128, 1]),
                       ("m2", [128, 1]), ("var", [128, 1]), ("rstd", [128, 1]),
                       ("mrs", [128, 1]), ("sq", [128, D]), ("hm", [128, H]), ("hr", [128, H])]:
            a.sb(nm, sh)
        for nm, sh in [("pT", [128, 128]), ("pA", [128, 512]), ("pB", [128, 512])]:
            a.ps(nm, sh)
        xt, xln, xlnT, wtile, ident, knwt, stat, ktile = (
            a.xt, a.xln, a.xlnT, a.wtile, a.ident, a.knwt, a.stat, a.ktile)
        mu, m2, var, rstd, mrs, sq, hm, hr = a.mu, a.m2, a.var, a.rstd, a.mrs, a.sq, a.hm, a.hr
        pT, pA, pB = a.pT, a.pA, a.pB
        block = es.enter_context(nc.Block())
        sm = Sems(nc, ["sin", "sv", "ss", "stn", "sout"])
        sm.clear_then_barrier(block)

        @block.sync
        def _(sync):
            sm.inc(sync.dma_start(out=wtile.ap().rearrange("p (c o) -> p c o", c=8),
                                  in_=WT.rearrange("(c p) o -> p c o", p=128)), "sin", 16)
            sm.inc(sync.dma_start(out=ident[:], in_=ident_d[:, :]), "sin", 16)
            sm.inc(sync.dma_start(out=knwt[:], in_=knw[None, :].to_broadcast([128, D])), "sin", 16)

        for i in range(NT):
            vafter_prev_use = sm.c["stn"]

            @block.sync
            def _(sync, i=i, vprev=vafter_prev_use):
                if i >= 1:
                    sm.wait(sync, "sv")
                sm.inc(sync.dma_start(out=xt[:], in_=x[i * 128:(i + 1) * 128, :]), "sin", 16)

            @block.vector
            def _(vector, i=i):
                sm.wait(vector, "sin")
                if i >= 1:
                    sm.wait(vector, "stn")
                    sm.wait(vector, "sout")

            emit_ln(block.vector, block.scalar, sm, xt[:], xln[:], D,
                    mu, m2, var, rstd, mrs, sq, LN_EPS, "sv", "ss")

            @block.vector
            def _(vector, i=i):
                vector.drain()
                vector.tensor_mul(sq[:], xln[:], xln[:])
                vector.drain()
                vector.reduce_sum(hm.ap().unsqueeze(2),
                                  sq.ap().rearrange("p (h d) -> p h d", h=H), axis=AX)
                vector.drain()
                vector.tensor_scalar(hm[:], hm[:], 1.0 / DH, RMS_EPS, ALU.mult, ALU.add)
                vector.drain()
                sm.inc(vector.reciprocal(hm[:], hm[:]), "sv")

            @block.scalar
            def _(scalar):
                sm.wait(scalar, "sv")
                sm.inc(scalar.activation(hr[:], hm[:], AF.Sqrt), "ss")

            @block.vector
            def _(vector):
                sm.wait(vector, "ss")
                vector.tensor_tensor(
                    ktile.ap().rearrange("p (h d) -> p h d", h=H),
                    xln.ap().rearrange("p (h d) -> p h d", h=H),
                    hr.ap().unsqueeze(2).to_broadcast([128, H, DH]),
                    ALU.mult)
                vector.drain()
                sm.inc(vector.tensor_tensor(ktile[:], ktile[:], knwt[:], ALU.mult), "sv")

            kdone = sm.c["sv"]

            @block.gpsimd
            def _(gpsimd, kd=kdone):
                sm.wait(gpsimd, "sv", kd)
                sm.inc(gpsimd.dma_start(out=kout[i * 128:(i + 1) * 128, :], in_=ktile[:]), "sout", 16)

            for c in range(8):
                @block.tensor
                def _(tensor, i=i, c=c):
                    if i == 0 and c == 0:
                        sm.wait(tensor, "sin")
                    sm.wait(tensor, "sv")
                    sm.wait(tensor, "ss")
                    sm.inc(tensor.transpose(pT[:, 0:128], xln[:, c * 128:(c + 1) * 128], ident[:]), "stn")

                @block.scalar
                def _(scalar, c=c):
                    sm.wait(scalar, "stn")
                    sm.inc(scalar.copy(xlnT[:, c * 128:(c + 1) * 128], pT[:]), "ss")

            @block.tensor
            def _(tensor, i=i):
                sm.wait(tensor, "ss")
                if i >= 1:
                    sm.wait(tensor, "sv")
                for half, pb in ((0, pA), (1, pB)):
                    for c in range(8):
                        ins = tensor.matmul(
                            pb[:], xlnT[:, c * 128:(c + 1) * 128],
                            wtile[:, c * D + half * 512: c * D + half * 512 + 512],
                            start=(c == 0), stop=(c == 7))
                sm.inc(ins, "stn")

            @block.vector
            def _(vector, i=i):
                sm.wait(vector, "stn")
                if i >= 1:
                    sm.wait(vector, "sout")
                vector.tensor_copy(stat[:, 0:512], pA[:])
                sm.inc(vector.tensor_copy(stat[:, 512:1024], pB[:]), "sv")

            @block.gpsimd
            def _(gpsimd, i=i):
                sm.wait(gpsimd, "sv")
                sm.inc(gpsimd.dma_start(out=static[i * 128:(i + 1) * 128, :], in_=stat[:]), "sout", 16)

        @block.gpsimd
        def _(gpsimd):
            sm.wait(gpsimd, "sout")
    return nc


def build_l2(nch=64):
    nc = bass.Bass("TRN2", target_bir_lowering=False, detect_race_conditions=False)
    k4 = nc.dram_tensor("k4", [nch, 128, 4, DH], F32, kind="ExternalInput").ap()
    v4 = nc.dram_tensor("v4", [nch, 128, 4, DH], F32, kind="ExternalInput").ap()
    lam4 = nc.dram_tensor("lam4", [4], F32, kind="ExternalInput").ap()
    eta4 = nc.dram_tensor("eta4", [4], F32, kind="ExternalInput").ap()
    o4 = nc.dram_tensor("o4", [nch, 128, 4, DH], F32, kind="ExternalOutput").ap()

    ii, jj = np.meshgrid(np.arange(128), np.arange(128), indexing="ij")
    iotacl_d = nc.inline_tensor(np.maximum(ii - jj, 0).astype(np.float32), "iotacl_d").ap()
    strict_d = nc.inline_tensor((jj < ii).astype(np.float32), "strict01_d").ap()
    ident_d = nc.inline_tensor(np.eye(128, dtype=np.float32), "identc_d").ap()
    ioi_d = nc.inline_tensor((np.arange(128, dtype=np.float32) + 1).reshape(128, 1), "ioi_d").ap()
    ior_d = nc.inline_tensor((127.0 - np.arange(128, dtype=np.float32)).reshape(128, 1), "ior_d").ap()
    c128_d = nc.inline_tensor(np.full((64, 1), 128.0, dtype=np.float32), "c128_d").ap()
    ones1_d = nc.inline_tensor(np.ones((1, 128), dtype=np.float32), "ones1_d").ap()

    NLEV = 7
    with contextlib.ExitStack() as es:
        a = Alloc(nc, es)
        for nm, sh in [("iotact", [128, 128]), ("strict", [128, 128]), ("identt", [128, 128]),
                       ("ioi", [128, 1]), ("ior", [128, 1]), ("c128", [64, 1]), ("ones1", [1, 128]),
                       ("vals", [1, 16]), ("tmp4", [1, 4]), ("bct", [128, 16]),
                       ("kk", [128, DH]), ("av", [128, 1])]:
            a.sb(nm, sh)
        for b in range(2):
            a.sb(f"kv{b}", [128, 4 * DH])
            a.sb(f"vv{b}", [128, 4 * DH])
            a.sb(f"ot{b}", [128, 4 * DH])
        for s in range(4):
            a.sb(f"mskL{s}", [128, 128]); a.sb(f"mskU{s}", [128, 128])
            a.sb(f"MUa{s}", [128, 192]); a.sb(f"MUb{s}", [128, 192])
            a.sb(f"Mta{s}", [128, 128]); a.sb(f"Mtb{s}", [128, 128])
            a.sb(f"kdT{s}", [64, 128]); a.sb(f"Ubf{s}", [128, DH]); a.sb(f"Ks{s}", [128, DH])
            a.sb(f"Sa{s}", [64, DH]); a.sb(f"Sb{s}", [64, DH])
            a.sb(f"nli{s}", [128, 1]); a.sb(f"bkr{s}", [128, 1]); a.sb(f"lC{s}", [64, 1])
            a.sb(f"gv{s}", [128, 1])
        a.ps("pTR", [64, 128]); a.ps("pA", [128, 128]); a.ps("pW", [128, DH])
        a.ps("pS", [64, DH])
        a.ps("pF0", [128, 192]); a.ps("pF1", [128, 192])
        a.ps("pM0", [128, 128]); a.ps("pM1", [128, 128])
        block = es.enter_context(nc.Block())
        sm = Sems(nc, ["sin", "sv", "ss", "stn", "sout"])
        sm.clear_then_barrier(block)

        @block.sync
        def _(sync):
            for dst, src_ in [(a.iotact, iotacl_d), (a.strict, strict_d), (a.identt, ident_d)]:
                sm.inc(sync.dma_start(out=dst[:], in_=src_[:, :]), "sin", 16)
            sm.inc(sync.dma_start(out=a.ioi[:], in_=ioi_d[:, :]), "sin", 16)
            sm.inc(sync.dma_start(out=a.ior[:], in_=ior_d[:, :]), "sin", 16)
            sm.inc(sync.dma_start(out=a.c128[:], in_=c128_d[:, :]), "sin", 16)
            sm.inc(sync.dma_start(out=a.ones1[:], in_=ones1_d[:, :]), "sin", 16)
            sm.inc(sync.dma_start(out=a.vals[:, 0:4], in_=lam4[None, :]), "sin", 16)
            sm.inc(sync.dma_start(out=a.tmp4[:], in_=eta4[None, :]), "sin", 16)
            sm.inc(sync.dma_start(out=a.kv0.ap().rearrange("p (s d) -> p s d", s=4),
                                  in_=k4[0]), "sin", 16)
            sm.inc(sync.dma_start(out=a.vv0.ap().rearrange("p (s d) -> p s d", s=4),
                                  in_=v4[0]), "sin", 16)

        @block.scalar
        def _(scalar):
            sm.wait(scalar, "sin")
            scalar.activation(a.vals[:, 0:4], a.vals[:, 0:4], AF.Sigmoid)
            scalar.drain()
            scalar.activation(a.tmp4[:], a.tmp4[:], AF.Tanh)
            scalar.drain()
            sm.inc(scalar.activation(a.vals[:, 4:8], a.vals[:, 0:4], AF.Ln), "ss")

        @block.vector
        def _(vector):
            sm.wait(vector, "ss")
            vector.tensor_scalar(a.vals[:, 8:12], a.vals[:, 0:4], -1.0, 1.0, ALU.mult, ALU.add)
            vector.drain()
            vector.tensor_mul(a.vals[:, 8:12], a.vals[:, 8:12], a.tmp4[:])
            vector.drain()
            vector.tensor_scalar_mul(a.vals[:, 12:16], a.vals[:, 8:12], 1.0 / DH)
            vector.drain()
            sm.inc(vector.tensor_scalar_mul(a.vals[:, 8:12], a.vals[:, 8:12], -1.0 / DH), "sv")

        @block.tensor
        def _(tensor):
            sm.wait(tensor, "sv")
            sm.wait(tensor, "sin")
            sm.inc(tensor.matmul(a.pA[:, 0:16], a.ones1[:], a.vals[:], start=True, stop=True), "stn")

        @block.scalar
        def _(scalar):
            sm.wait(scalar, "stn")
            sm.inc(scalar.copy(a.bct[:], a.pA[:, 0:16]), "ss")

        for s in range(4):
            lnl = a.bct[:, 4 + s:5 + s]

            @block.scalar
            def _(scalar, s=s, lnl=lnl):
                sm.wait(scalar, "ss")
                scalar.activation(getattr(a, f"mskL{s}")[:], a.iotact[:], AF.Exp, scale=lnl)
                scalar.activation(getattr(a, f"nli{s}")[:], a.ioi[:], AF.Exp, scale=lnl)
                scalar.activation(getattr(a, f"bkr{s}")[:], a.ior[:], AF.Exp, scale=lnl)
                sm.inc(scalar.activation(getattr(a, f"lC{s}")[:], a.c128[:], AF.Exp,
                                         scale=lnl[0:64, :]), "ss")
                scalar.drain()
                scalar.mul(getattr(a, f"nli{s}")[:], getattr(a, f"nli{s}")[:], -1.0)
                sm.inc(scalar.mul(getattr(a, f"bkr{s}")[:], getattr(a, f"bkr{s}")[:],
                                  a.bct[:, 8 + s:9 + s]), "ss")

            @block.vector
            def _(vector, s=s):
                sm.wait(vector, "ss")
                vector.tensor_tensor(getattr(a, f"mskL{s}")[:], getattr(a, f"mskL{s}")[:],
                                     a.strict[:], ALU.mult)
                vector.drain()
                vector.tensor_scalar(getattr(a, f"mskL{s}")[:], getattr(a, f"mskL{s}")[:],
                                     a.bct[:, 12 + s:13 + s], None, ALU.mult)
                vector.memset(getattr(a, f"Sa{s}")[:], 0.0)
                sm.inc(vector.memset(getattr(a, f"Sb{s}")[:], 0.0), "sv")

            @block.tensor
            def _(tensor, s=s):
                sm.wait(tensor, "sv")
                sm.inc(tensor.transpose(a.pM0[:], getattr(a, f"mskL{s}")[:], a.identt[:]), "stn")

            @block.scalar
            def _(scalar, s=s):
                sm.wait(scalar, "stn")
                sm.inc(scalar.copy(getattr(a, f"mskU{s}")[:], a.pM0[:]), "ss")

        rd = {"pTR": 0, "pA": 0, "pW": 0, "pS": 0,
              "pF": [(0, 0), (0, 0)], "pM": [0, 0]}
        load_done = [0] * (nch + 1)
        load_done[0] = sm.c["sin"]
        sdone = [0] * 4
        sdone_mm = [0] * 4
        last = {"kv_read": [0, 0], "kvv_read": [0, 0], "kvs_read": [0, 0], "ot_st": [0, 0]}

        for c in range(nch):
            buf = c % 2
            kv = getattr(a, f"kv{buf}"); vv = getattr(a, f"vv{buf}"); ot = getattr(a, f"ot{buf}")

            if c + 1 < nch:
                nbuf = (c + 1) % 2

                @block.sync
                def _(sync, c=c, nbuf=nbuf):
                    sm.wait(sync, "stn", last["kv_read"][nbuf])
                    sm.wait(sync, "sv", last["kvv_read"][nbuf])
                    sm.wait(sync, "ss", last["kvs_read"][nbuf])
                    sm.inc(sync.dma_start(out=getattr(a, f"kv{nbuf}").ap().rearrange("p (s d) -> p s d", s=4),
                                          in_=k4[c + 1]), "sin", 16)
                    sm.inc(sync.dma_start(out=getattr(a, f"vv{nbuf}").ap().rearrange("p (s d) -> p s d", s=4),
                                          in_=v4[c + 1]), "sin", 16)

                load_done[c + 1] = sm.c["sin"]

            ST = []
            for s in range(4):
                ST.append(dict(
                    P=s % 2,
                    pF=(a.pF0, a.pF1)[s % 2], pM=(a.pM0, a.pM1)[s % 2],
                    mskL=getattr(a, f"mskL{s}"), mskU=getattr(a, f"mskU{s}"),
                    MU=[getattr(a, f"MUa{s}"), getattr(a, f"MUb{s}")],
                    Mt=[getattr(a, f"Mta{s}"), getattr(a, f"Mtb{s}")],
                    kdT=getattr(a, f"kdT{s}"), Ubf=getattr(a, f"Ubf{s}"),
                    Ks=getattr(a, f"Ks{s}"), gv=getattr(a, f"gv{s}"),
                    Scur=getattr(a, f"Sa{s}") if c % 2 == 0 else getattr(a, f"Sb{s}"),
                    Snxt=getattr(a, f"Sb{s}") if c % 2 == 0 else getattr(a, f"Sa{s}"),
                    kap=kv[:, s * DH:(s + 1) * DH], vap=vv[:, s * DH:(s + 1) * DH],
                ))

            for s in range(4):
                st = ST[s]

                @block.vector
                def _(vector, c=c, s=s, st=st):
                    if s == 0:
                        sm.wait(vector, "sin", load_done[c])
                    vector.tensor_mul(a.kk[:], st["kap"], st["kap"])
                    vector.drain()
                    vector.reduce_sum(a.av[:], a.kk[:], axis=AX)
                    vector.drain()
                    sm.inc(vector.tensor_scalar(st["gv"][:], a.av[:], a.bct[:, 12 + s:13 + s], 1.0,
                                                ALU.mult, ALU.add), "sv")

                st["v_g"] = sm.c["sv"]

                @block.scalar
                def _(scalar, c=c, s=s, st=st):
                    if s == 0:
                        sm.wait(scalar, "sin", load_done[c])
                    sm.wait(scalar, "stn", sdone_mm[s])
                    sm.inc(scalar.mul(st["Ks"][:], st["kap"], getattr(a, f"bkr{s}")[:]), "ss")

                st["s_ks"] = sm.c["ss"]

            for s in range(4):
                st = ST[s]

                @block.tensor
                def _(tensor, c=c, s=s, st=st):
                    if s == 0:
                        sm.wait(tensor, "sin", load_done[c])
                    sm.wait(tensor, "sv", rd["pTR"])
                    sm.inc(tensor.transpose(a.pTR[:], st["kap"], a.identt[:]), "stn")

                t_tr = sm.c["stn"]

                @block.vector
                def _(vector, t_tr=t_tr, st=st):
                    sm.wait(vector, "stn", t_tr)
                    sm.inc(vector.tensor_copy(st["kdT"][:], a.pTR[:]), "sv")

                st["s_kdT"] = sm.c["sv"]
                rd["pTR"] = st["s_kdT"]

            for s in range(4):
                st = ST[s]

                @block.tensor
                def _(tensor, c=c, s=s, st=st):
                    sm.wait(tensor, "sv", max(st["s_kdT"], rd["pA"], rd["pW"], sdone[s]))
                    sm.inc(tensor.matmul(a.pA[:], st["kdT"][:], st["kdT"][:], start=True, stop=True), "stn")
                    sm.inc(tensor.matmul(a.pW[:], st["kdT"][:], st["Scur"][:], start=True, stop=True), "stn")

                t_AW = sm.c["stn"]

                @block.vector
                def _(vector, c=c, s=s, st=st, t_AW=t_AW):
                    sm.wait(vector, "stn", t_AW)
                    vector.tensor_tensor(st["MU"][0][:, 0:128], a.pA[:], st["mskL"][:], ALU.mult)
                    vector.tensor_tensor(st["Mt"][0][:], a.pA[:], st["mskU"][:], ALU.mult)
                    sm.inc(vector.scalar_tensor_tensor(
                        st["MU"][0][:, 128:192], a.pW[:], getattr(a, f"nli{s}")[:],
                        st["vap"], ALU.mult, ALU.add), "sv")

                st["v_mu"] = sm.c["sv"]
                rd["pA"] = sm.c["sv"]
                rd["pW"] = sm.c["sv"]
                st["s_m"] = 0
                st["s_mt"] = 0

            for lev in range(NLEV - 1):
                cur = lev % 2
                nxt = 1 - cur
                for s in range(4):
                    st = ST[s]
                    P = st["P"]
                    pF = st["pF"]; pM = st["pM"]; MU = st["MU"]; Mt = st["Mt"]

                    @block.tensor
                    def _(tensor, st=st, P=P, pF=pF, pM=pM, MU=MU, Mt=Mt, cur=cur):
                        fs, fv = rd["pF"][P]
                        sm.wait(tensor, "sv", max(st["v_mu"], fv, st["s_m"], st["s_mt"],
                                                  rd["pM"][P], fs))
                        sm.inc(tensor.matmul(pF[:], Mt[cur][:], MU[cur][:, 0:192],
                                             start=True, stop=True), "stn")
                        sm.inc(tensor.matmul(pM[:], MU[cur][:, 0:128], Mt[cur][:],
                                             start=True, stop=True), "stn")

                    t_m2 = sm.c["stn"]
                    t_f = t_m2 - 1

                    @block.vector
                    def _(vector, cur=cur, nxt=nxt, t_f=t_f, pF=pF, MU=MU):
                        sm.wait(vector, "stn", t_f)
                        vector.tensor_copy(MU[nxt][:, 0:128], pF[:, 0:128])
                        sm.inc(vector.tensor_add(MU[nxt][:, 128:192], MU[cur][:, 128:192],
                                                 pF[:, 128:192]), "sv")

                    st["s_m"] = sm.c["sv"]
                    st["v_mu"] = sm.c["sv"]

                    @block.vector
                    def _(vector, nxt=nxt, t_m2=t_m2, pM=pM, Mt=Mt):
                        sm.wait(vector, "stn", t_m2)
                        sm.inc(vector.tensor_copy(Mt[nxt][:], pM[:]), "sv")

                    st["s_mt"] = sm.c["sv"]
                    rd["pF"][P] = (0, st["v_mu"])
                    rd["pM"][P] = st["s_mt"]

            for s in range(4):
                st = ST[s]
                cur = (NLEV - 1) % 2
                MU = st["MU"]; Mt = st["Mt"]

                @block.tensor
                def _(tensor, st=st, cur=cur, MU=MU, Mt=Mt):
                    sm.wait(tensor, "sv", max(st["v_mu"], rd["pW"], st["s_m"], st["s_mt"]))
                    sm.inc(tensor.matmul(a.pW[:], Mt[cur][:], MU[cur][:, 128:192],
                                         start=True, stop=True), "stn")

                t_fin = sm.c["stn"]

                @block.vector
                def _(vector, st=st, cur=cur, t_fin=t_fin, MU=MU, c=c, s=s, buf=buf, ot=ot):
                    sm.wait(vector, "stn", t_fin)
                    vector.tensor_add(st["Ubf"][:], MU[cur][:, 128:192], a.pW[:])
                    if c >= 2:
                        sm.wait(vector, "sout", last["ot_st"][buf])
                    vector.drain()
                    sm.inc(vector.scalar_tensor_tensor(
                        ot[:, s * DH:(s + 1) * DH], st["Ubf"][:], st["gv"][:], st["vap"],
                        ALU.mult, ALU.subtract), "sv")

                st["v_ubf"] = sm.c["sv"]
                rd["pW"] = sm.c["sv"]

                @block.tensor
                def _(tensor, st=st):
                    sm.wait(tensor, "ss", st["s_ks"])
                    sm.wait(tensor, "sv", max(st["v_ubf"], rd["pS"]))
                    sm.inc(tensor.matmul(a.pS[:], st["Ks"][:], st["Ubf"][:], start=True, stop=True), "stn")

                t_s = sm.c["stn"]
                sdone_mm[s] = t_s

                @block.vector
                def _(vector, t_s=t_s, s=s, st=st):
                    sm.wait(vector, "stn", t_s)
                    sm.inc(vector.scalar_tensor_tensor(
                        st["Snxt"][:], st["Scur"][:], getattr(a, f"lC{s}")[:], a.pS[:],
                        ALU.mult, ALU.add), "sv")

                rd["pS"] = sm.c["sv"]
                sdone[s] = sm.c["sv"]

            v_all_o = sm.c["sv"]

            @block.gpsimd
            def _(gpsimd, c=c, buf=buf, v_all_o=v_all_o, ot=ot):
                sm.wait(gpsimd, "sv", v_all_o)
                sm.inc(gpsimd.dma_start(out=o4[c], in_=ot.ap().rearrange("p (s d) -> p s d", s=4)),
                       "sout", 16)

            last["ot_st"][buf] = sm.c["sout"]
            last["kv_read"][buf] = sm.c["stn"]
            last["kvv_read"][buf] = sm.c["sv"]
            last["kvs_read"][buf] = sm.c["ss"]

        @block.gpsimd
        def _(gpsimd):
            sm.wait(gpsimd, "sout")
    return nc


def build_l3():
    nc = bass.Bass("TRN2", target_bir_lowering=False, detect_race_conditions=False)
    x = nc.dram_tensor("x", [TC, D], F32, kind="ExternalInput").ap()
    st = nc.dram_tensor("st", [TC, D], F32, kind="ExternalInput").ap()
    od = nc.dram_tensor("od", [TC, D], F32, kind="ExternalInput").ap()
    ow = nc.dram_tensor("ow", [D], F32, kind="ExternalInput").ap()
    ob = nc.dram_tensor("ob", [D], F32, kind="ExternalInput").ap()
    wr = nc.dram_tensor("wr", [D, E], F32, kind="ExternalInput").ap()
    resid = nc.dram_tensor("resid", [TC, D], F32, kind="ExternalOutput").ap()
    xt2o = nc.dram_tensor("xt2o", [TC, D], F32, kind="ExternalOutput").ap()
    lgo = nc.dram_tensor("lgo", [TC, E], F32, kind="ExternalOutput").ap()
    ident_d = nc.inline_tensor(np.eye(128, dtype=np.float32), "identc_d").ap()

    NT = TC // 128
    with contextlib.ExitStack() as es:
        a = Alloc(nc, es)
        for nm, sh in [("xt", [128, D]), ("stt", [128, D]), ("odt", [128, D]),
                       ("z", [128, D]), ("lnb", [128, D]), ("xt2", [128, D]),
                       ("rez", [128, D]), ("xt2T", [128, D]), ("lgt", [128, E]),
                       ("ident", [128, 128]), ("owt", [128, D]), ("obt", [128, D]),
                       ("wrt", [128, 8 * E]), ("mu", [128, 1]), ("m2", [128, 1]),
                       ("var", [128, 1]), ("rstd", [128, 1]), ("mrs", [128, 1]),
                       ("sq", [128, D])]:
            a.sb(nm, sh)
        a.ps("pT", [128, 128]); a.ps("pL", [128, E])
        block = es.enter_context(nc.Block())
        sm = Sems(nc, ["sin", "sv", "ss", "stn", "sout"])
        sm.clear_then_barrier(block)

        @block.sync
        def _(sync):
            sm.inc(sync.dma_start(out=a.ident[:], in_=ident_d[:, :]), "sin", 16)
            sm.inc(sync.dma_start(out=a.owt[:], in_=ow[None, :].to_broadcast([128, D])), "sin", 16)
            sm.inc(sync.dma_start(out=a.obt[:], in_=ob[None, :].to_broadcast([128, D])), "sin", 16)
            sm.inc(sync.dma_start(out=a.wrt.ap().rearrange("p (c e) -> p c e", c=8),
                                  in_=wr.rearrange("(c p) e -> p c e", p=128)), "sin", 16)

        for i in range(NT):
            @block.sync
            def _(sync, i=i):
                if i >= 1:
                    sm.wait(sync, "sv")
                    sm.wait(sync, "stn")
                sm.inc(sync.dma_start(out=a.xt[:], in_=x[i * 128:(i + 1) * 128, :]), "sin", 16)
                sm.inc(sync.dma_start(out=a.stt[:], in_=st[i * 128:(i + 1) * 128, :]), "sin", 16)
                sm.inc(sync.dma_start(out=a.odt[:], in_=od[i * 128:(i + 1) * 128, :]), "sin", 16)

            @block.vector
            def _(vector, i=i):
                sm.wait(vector, "sin")
                if i >= 1:
                    sm.wait(vector, "sout")
                sm.inc(vector.tensor_add(a.z[:], a.stt[:], a.odt[:]), "sv")

            @block.vector
            def _(vector):
                vector.drain()
            emit_ln(block.vector, block.scalar, sm, a.z[:], a.lnb[:], D,
                    a.mu, a.m2, a.var, a.rstd, a.mrs, a.sq, LN_EPS, "sv", "ss")

            @block.vector
            def _(vector):
                vector.drain()
                vector.tensor_mul(a.lnb[:], a.lnb[:], a.owt[:])
                vector.drain()
                vector.tensor_add(a.lnb[:], a.lnb[:], a.obt[:])
                vector.drain()
                sm.inc(vector.tensor_add(a.rez[:], a.xt[:], a.lnb[:]), "sv")

            @block.vector
            def _(vector):
                vector.drain()
            emit_ln(block.vector, block.scalar, sm, a.rez[:], a.xt2[:], D,
                    a.mu, a.m2, a.var, a.rstd, a.mrs, a.sq, LN_EPS, "sv", "ss")
            v_xt2 = sm.c["sv"]

            for c in range(8):
                @block.tensor
                def _(tensor, c=c):
                    sm.wait(tensor, "sv")
                    sm.wait(tensor, "ss")
                    sm.wait(tensor, "sin")
                    sm.inc(tensor.transpose(a.pT[:], a.xt2[:, c * 128:(c + 1) * 128], a.ident[:]), "stn")

                @block.scalar
                def _(scalar, c=c):
                    sm.wait(scalar, "stn")
                    sm.inc(scalar.copy(a.xt2T[:, c * 128:(c + 1) * 128], a.pT[:]), "ss")

            @block.tensor
            def _(tensor, i=i):
                sm.wait(tensor, "ss")
                if i >= 1:
                    sm.wait(tensor, "sv")
                for c in range(8):
                    ins = tensor.matmul(a.pL[:], a.xt2T[:, c * 128:(c + 1) * 128],
                                        a.wrt[:, c * E:(c + 1) * E],
                                        start=(c == 0), stop=(c == 7))
                sm.inc(ins, "stn")

            @block.vector
            def _(vector):
                sm.wait(vector, "stn")
                sm.inc(vector.tensor_copy(a.lgt[:], a.pL[:]), "sv")

            v_all = sm.c["sv"]

            @block.gpsimd
            def _(gpsimd, i=i, v_all=v_all, v_xt2=v_xt2):
                sm.wait(gpsimd, "sv", v_all)
                sm.inc(gpsimd.dma_start(out=resid[i * 128:(i + 1) * 128, :], in_=a.rez[:]), "sout", 16)
                sm.inc(gpsimd.dma_start(out=xt2o[i * 128:(i + 1) * 128, :], in_=a.xt2[:]), "sout", 16)
                sm.inc(gpsimd.dma_start(out=lgo[i * 128:(i + 1) * 128, :], in_=a.lgt[:]), "sout", 16)

        @block.gpsimd
        def _(gpsimd):
            sm.wait(gpsimd, "sout")
    return nc


def build_l6():
    nc = bass.Bass("TRN2", target_bir_lowering=False, detect_race_conditions=False)
    rz = nc.dram_tensor("rz", [TC, D], F32, kind="ExternalInput").ap()
    xt2 = nc.dram_tensor("xt2", [TC, D], F32, kind="ExternalInput").ap()
    yfull = nc.dram_tensor("yfull", [EC, D], F32, kind="ExternalInput").ap()
    slot = nc.dram_tensor("slot", [TC], F32, kind="ExternalInput").ap()
    gate = nc.dram_tensor("gate", [TC], F32, kind="ExternalInput").ap()
    out = nc.dram_tensor("out", [TC, D], F32, kind="ExternalOutput").ap()

    NT = TC // 128
    with contextlib.ExitStack() as es:
        a = Alloc(nc, es)
        for nm, sh in [("rzt", [128, D]), ("xtt", [128, D]), ("yt", [128, D]),
                       ("ot", [128, D]), ("gt", [128, 1]), ("sl", [128, 1])]:
            a.sb(nm, sh)
        a.sb("sli", [128, 1], I32)
        block = es.enter_context(nc.Block())
        sm = Sems(nc, ["sin", "sv", "sg", "sout"])
        sm.clear_then_barrier(block)

        for i in range(NT):
            @block.sync
            def _(sync, i=i):
                if i >= 1:
                    sm.wait(sync, "sv")
                sm.inc(sync.dma_start(out=a.rzt[:], in_=rz[i * 128:(i + 1) * 128, :]), "sin", 16)
                sm.inc(sync.dma_start(out=a.xtt[:], in_=xt2[i * 128:(i + 1) * 128, :]), "sin", 16)
                sm.inc(sync.dma_start(out=a.gt[:], in_=gate[i * 128:(i + 1) * 128, None]), "sin", 16)
                sm.inc(sync.dma_start(out=a.sl[:], in_=slot[i * 128:(i + 1) * 128, None]), "sin", 16)

            @block.vector
            def _(vector, i=i):
                sm.wait(vector, "sin")
                if i >= 1:
                    sm.wait(vector, "sg")
                sm.inc(vector.tensor_copy(a.sli[:], a.sl[:]), "sv")

            @block.gpsimd
            def _(gpsimd, i=i):
                sm.wait(gpsimd, "sv")
                sm.inc(gpsimd.indirect_dma_start(
                    a.yt[:], None, yfull[:, :],
                    IndirectOffsetOnAxis(a.sli[:], 0),
                    bounds_check=EC - 1, oob_is_err=False), "sg", 16)

            @block.vector
            def _(vector, i=i):
                sm.wait(vector, "sg")
                if i >= 1:
                    sm.wait(vector, "sout")
                vector.scalar_tensor_tensor(a.ot[:], a.yt[:], a.gt[:], a.rzt[:],
                                            ALU.mult, ALU.add)
                vector.drain()
                sm.inc(vector.tensor_add(a.ot[:], a.ot[:], a.xtt[:]), "sv")

            @block.gpsimd
            def _(gpsimd, i=i):
                sm.wait(gpsimd, "sv")
                sm.inc(gpsimd.dma_start(out=out[i * 128:(i + 1) * 128, :], in_=a.ot[:]), "sout", 16)

        @block.gpsimd
        def _(gpsimd):
            sm.wait(gpsimd, "sout")
    return nc


def build_l5():
    nc = bass.Bass("TRN2", target_bir_lowering=False, detect_race_conditions=False)
    lg_d = nc.dram_tensor("lg", [T, E], F32, kind="ExternalInput").ap()
    xt_d = nc.dram_tensor("xt", [T, D], F32, kind="ExternalInput").ap()
    wg_d = nc.dram_tensor("wg4", [4, D, D], F32, kind="ExternalInput").ap()
    wu_d = nc.dram_tensor("wu4", [4, D, D], F32, kind="ExternalInput").ap()
    wd_d = nc.dram_tensor("wd4", [4, D, D], F32, kind="ExternalInput").ap()
    soff_d = nc.dram_tensor("soff", [128, 1], F32, kind="ExternalInput").ap()
    y4_d = nc.dram_tensor("y4", [4 * CCAP, D], F32, kind="ExternalOutput").ap()
    slot_d = nc.dram_tensor("slot", [T], F32, kind="ExternalOutput").ap()
    gate_d = nc.dram_tensor("gate", [T], F32, kind="ExternalOutput").ap()
    inv_d = nc.dram_tensor("inv", [4 * CCAP, 1], I32, kind="Internal").ap()

    ident_d = nc.inline_tensor(np.eye(128, dtype=np.float32), "identc_d").ap()
    ones_d = nc.inline_tensor(np.ones((128, 128), dtype=np.float32), "ones_d").ap()
    ioe_d = nc.inline_tensor(np.tile(np.arange(E, dtype=np.float32), (128, 1)), "ioe_d").ap()
    ioe100_d = nc.inline_tensor(np.tile(np.arange(E, dtype=np.float32) - 100.0, (128, 1)), "ioe100_d").ap()
    tokc_np = (np.arange(128)[:, None] * 128 + np.arange(128)[None, :]).astype(np.int32)
    tokc_d = nc.inline_tensor(tokc_np, "tokc_d").ap()

    NF = T // 128 // 128
    with contextlib.ExitStack() as es:
        a = Alloc(nc, es)
        for p in range(4):
            a.sb(f"pool{p}", [128, 5120])
        for nm, sh in [("ident", [128, 128]), ("ones", [128, 128]), ("ioe", [128, E]),
                       ("ioe100", [128, E]), ("lmax", [128, 128]), ("ssum", [128, 128]),
                       ("gatee", [128, 128]), ("top1", [128, 128]), ("tot", [128, E]),
                       ("totT", [32, 128]), ("cum", [32, 128]), ("excl", [32, 128]),
                       ("off", [128, E]), ("posi", [128, 128]), ("acc", [128, 128]),
                       ("slotf", [128, 128]), ("slotl", [128, 128]), ("sofft", [128, 1]), ("acc2", [128, 128]),
                       ("wbuf", [128, 8 * D]), ("wbuf2", [128, 8 * D]),
                       ("btile", [128, D]), ("btile2", [128, D]),
                       ("ytile", [128, D]), ("ytile2", [128, D])]:
            a.sb(nm, sh)
        a.sb("tokc", [128, 128], I32)
        a.sb("pad", [128, 20], I32)
        a.sb("sli", [128, 128], I32)
        a.sb("gidx", [128, 1], I32)
        a.sb("gidx2", [128, 1], I32)
        a.ps("pTs", [128, 128]); a.ps("pTs1", [128, 128]); a.ps("pTr", [32, 128]); a.ps("pTr2", [128, 32])
        a.ps("pm0", [128, 320]); a.ps("pm1", [128, 320])
        a.ps("pm2", [128, 320]); a.ps("pm3", [128, 320])
        block = es.enter_context(nc.Block())
        sm = Sems(nc, ["sin", "sv", "ss", "stn", "sg", "sout"])
        sm.clear_then_barrier(block)

        lgv = a.pool0.ap()[:, 0:4096].rearrange("p (f e) -> p f e", e=E)
        ohv = a.pool1.ap()[:, 0:4096].rearrange("p (e f) -> p e f", e=E)
        cntv = a.pool2.ap()[:, 0:4096].rearrange("p (e f) -> p e f", e=E)
        t1v = a.pool3.ap()[:, 0:4096].rearrange("p (e f) -> p e f", e=E)

        @block.sync
        def _(sync):
            for dst, src in [(a.ident, ident_d), (a.ones, ones_d), (a.ioe, ioe_d),
                             (a.ioe100, ioe100_d), (a.tokc, tokc_d), (a.sofft, soff_d)]:
                sm.inc(sync.dma_start(out=dst[:], in_=src[:, :]), "sin", 16)
            sm.inc(sync.dma_start(out=lgv, in_=lg_d.rearrange("(p f) e -> p f e", p=128)), "sin", 16)

        @block.vector
        def _(vector):
            sm.wait(vector, "sin")
            vector.reduce_max(a.lmax[:], lgv, axis=AX)
            vector.drain()
            sm.inc(vector.tensor_tensor(
                a.pool1.ap()[:, 0:4096].rearrange("p (f e) -> p f e", e=E), lgv,
                a.lmax.ap().unsqueeze(2).to_broadcast([128, 128, E]), ALU.subtract), "sv")

        @block.scalar
        def _(scalar):
            sm.wait(scalar, "sv")
            sm.inc(scalar.activation(a.pool1[:, 0:4096], a.pool1[:, 0:4096], AF.Exp), "ss")

        @block.vector
        def _(vector):
            sm.wait(vector, "ss")
            vector.reduce_sum(a.ssum.ap().unsqueeze(2),
                              a.pool1.ap()[:, 0:4096].rearrange("p (f e) -> p f e", e=E), axis=AX)
            vector.drain()
            vector.reciprocal(a.gatee[:], a.ssum[:])
            vector.tensor_tensor(
                a.pool1.ap()[:, 0:4096].rearrange("p (f e) -> p f e", e=E), lgv,
                a.lmax.ap().unsqueeze(2).to_broadcast([128, 128, E]), ALU.is_ge)
            vector.drain()
            vector.tensor_tensor(
                a.pool1.ap()[:, 0:4096].rearrange("p (f e) -> p f e", e=E),
                a.pool1.ap()[:, 0:4096].rearrange("p (f e) -> p f e", e=E),
                a.ioe100.ap().unsqueeze(1).to_broadcast([128, 128, E]), ALU.mult)
            vector.drain()
            vector.tensor_reduce(a.top1.ap().unsqueeze(2),
                                 a.pool1.ap()[:, 0:4096].rearrange("p (f e) -> p f e", e=E),
                                 AX, ALU.min)
            vector.drain()
            vector.tensor_scalar_add(a.top1[:], a.top1[:], 100.0)
            vector.drain()
            sm.inc(vector.tensor_tensor(
                ohv, a.top1.ap().unsqueeze(1).to_broadcast([128, E, 128]),
                a.ioe.ap().unsqueeze(2).to_broadcast([128, E, 128]), ALU.is_equal), "sv")

        @block.vector
        def _(vector):
            for e in range(E):
                vector.tensor_tensor_scan(cntv[:, e, :], a.ones[:], ohv[:, e, :],
                                          0.0, ALU.mult, ALU.add)
            vector.drain()
            sm.inc(vector.tensor_copy(a.tot.ap().unsqueeze(2), cntv[:, :, 127:128]), "sv")

        @block.tensor
        def _(tensor):
            sm.wait(tensor, "sv")
            sm.wait(tensor, "sin")
            sm.inc(tensor.transpose(a.pTr[:], a.tot[:], a.ident[:]), "stn")

        @block.scalar
        def _(scalar):
            sm.wait(scalar, "stn")
            sm.inc(scalar.copy(a.totT[:], a.pTr[:]), "ss")

        @block.vector
        def _(vector):
            sm.wait(vector, "ss")
            vector.tensor_tensor_scan(a.cum[:], a.ones[0:32, :], a.totT[:], 0.0,
                                      ALU.mult, ALU.add)
            vector.drain()
            vector.memset(a.excl[:, 0:1], 0.0)
            vector.tensor_copy(a.excl[:, 1:128], a.cum[:, 0:127])
            sm.inc(vector.drain(), "sv")

        @block.tensor
        def _(tensor):
            sm.wait(tensor, "sv")
            sm.inc(tensor.transpose(a.pTr2[:], a.excl[:], a.ident[0:32, 0:32]), "stn")

        @block.scalar
        def _(scalar):
            sm.wait(scalar, "stn")
            sm.inc(scalar.copy(a.off[:], a.pTr2[:]), "ss")

        @block.vector
        def _(vector):
            sm.wait(vector, "ss")
            vector.tensor_tensor(t1v, cntv,
                                 a.off.ap().unsqueeze(2).to_broadcast([128, E, 128]), ALU.add)
            vector.drain()
            vector.tensor_tensor(t1v, t1v, ohv, ALU.mult)
            vector.drain()
            vector.tensor_reduce(a.posi.ap().unsqueeze(2),
                                 a.pool3.ap()[:, 0:4096].rearrange("p (e f) -> p e f", e=E).transpose([0, 2, 1]),
                                 AX, ALU.add)
            vector.drain()
            vector.tensor_scalar(a.acc[:], a.posi[:], 641.0, None, ALU.is_lt)
            vector.scalar_tensor_tensor(a.slotf[:], a.top1[:], 640.0, a.posi[:],
                                        ALU.mult, ALU.add)
            vector.drain()
            vector.tensor_scalar_add(a.slotf[:], a.slotf[:], -20481.0)
            vector.drain()
            vector.tensor_mul(a.slotf[:], a.slotf[:], a.acc[:])
            vector.drain()
            vector.tensor_scalar_add(a.slotf[:], a.slotf[:], 20480.0)
            vector.tensor_mul(a.gatee[:], a.gatee[:], a.acc[:])
            vector.drain()
            vector.scalar_tensor_tensor(a.slotl[:], a.sofft[:].to_broadcast([128, 128]),
                                        -1.0, a.slotf[:], ALU.mult, ALU.add)
            vector.drain()
            vector.tensor_scalar(a.acc2[:], a.slotl[:], 0.0, None, ALU.is_lt)
            vector.drain()
            vector.scalar_tensor_tensor(a.slotl[:], a.acc2[:], 30000.0, a.slotl[:],
                                        ALU.mult, ALU.add)
            vector.drain()
            vector.tensor_copy(a.sli[:], a.slotl[:])
            sm.inc(vector.drain(), "sv")

        v_route = sm.c["sv"]

        @block.gpsimd
        def _(gpsimd):
            sm.wait(gpsimd, "sv", v_route)
            sm.inc(gpsimd.dma_start(out=slot_d.rearrange("(p f) -> p f", p=128), in_=a.slotf[:]), "sout", 16)
            sm.inc(gpsimd.dma_start(out=gate_d.rearrange("(p f) -> p f", p=128), in_=a.gatee[:]), "sout", 16)
            gpsimd.memset(a.pad[:], 16384)
            sm.inc(gpsimd.dma_start(out=inv_d.rearrange("(p g) one -> p (g one)", p=128),
                                    in_=a.pad[:]), "sg", 16)
            sm.wait(gpsimd, "sg")
            bc = gpsimd.to_reg(4 * CCAP - 1)
            for j in range(128):
                sm.inc(gpsimd.indirect_dma_start(
                    inv_d[:, :], IndirectOffsetOnAxis(a.sli[:, j:j + 1], 0),
                    a.tokc[:, j:j + 1], None,
                    bounds_check=bc, oob_is_err=False), "sg", 16)
            sm.wait(gpsimd, "sg")

        g_scat = sm.c["sg"]
        bcg = [None]

        @block.gpsimd
        def _(gpsimd):
            bcg[0] = gpsimd.to_reg(T - 1)

        pms = [a.pm0, a.pm1, a.pm2, a.pm3]
        rpm = [("sv", 0)] * 4
        rpts = [0, 0]
        rbt = [0, 0]
        ryt = [0, 0]
        wdone = {}
        wread = {"g": 0, "u": 0, "d": 0}

        def emit_wload(e, nm, wt, b):
            wb = (a.wbuf, a.wbuf2)[b]

            @block.sync
            def _(sync, e=e, nm=nm, wt=wt, b=b, wb=wb):
                sm.wait(sync, "stn", wread.get(("buf", b), 0))
                pat = "(hc p) f -> p hc f" if nm != "d" else "(fc p) h -> p fc h"
                sm.inc(sync.dma_start(out=wb.ap().rearrange("p (hc f) -> p hc f", hc=8),
                                      in_=wt[e].rearrange(pat, p=128)), "sin", 16)

            wdone[(e, nm)] = sm.c["sin"]

        for e in range(4):
            emit_wload(e, "g", wg_d, (e * 3 + 0) % 2)
            emit_wload(e, "u", wu_d, (e * 3 + 1) % 2)
            wbg = (a.wbuf, a.wbuf2)[(e * 3 + 0) % 2]
            wbu = (a.wbuf, a.wbuf2)[(e * 3 + 1) % 2]
            wbd = (a.wbuf, a.wbuf2)[(e * 3 + 2) % 2]

            s_bufT_parts = []
            for t in range(5):
                bt = t % 2
                btile = (a.btile, a.btile2)[bt]

                gX = (a.gidx, a.gidx2)[bt]

                @block.gpsimd
                def _(gpsimd, e=e, t=t, bt=bt, btile=btile, gX=gX):
                    if e == 0 and t == 0:
                        sm.wait(gpsimd, "sg", g_scat)
                    sm.wait(gpsimd, "sg")
                    sm.inc(gpsimd.dma_start(out=gX[:], in_=inv_d[e * CCAP + t * 128:e * CCAP + (t + 1) * 128, :]), "sg", 16)
                    sm.wait(gpsimd, "sg")
                    sm.wait(gpsimd, "stn", rbt[bt])
                    sm.inc(gpsimd.indirect_dma_start(
                        btile[:], None, xt_d[:, :],
                        IndirectOffsetOnAxis(gX[:], 0),
                        bounds_check=bcg[0], oob_is_err=False), "sg", 16)

                g_gather = sm.c["sg"]
                for hc in range(8):
                    pb = hc % 2
                    pTs = (a.pTs, a.pTs1)[pb]

                    @block.tensor
                    def _(tensor, t=t, hc=hc, pb=pb, pTs=pTs, g_gather=g_gather, btile=btile):
                        if hc == 0:
                            sm.wait(tensor, "sg", g_gather)
                        sm.wait(tensor, "sv", rpts[pb])
                        sm.inc(tensor.transpose(pTs[:], btile[:, hc * 128:(hc + 1) * 128],
                                                a.ident[:]), "stn")

                    t_tr = sm.c["stn"]

                    @block.vector
                    def _(vector, t=t, hc=hc, pb=pb, pTs=pTs, t_tr=t_tr):
                        sm.wait(vector, "stn", max(t_tr, wread.get(("pool0",), 0)))
                        sm.inc(vector.tensor_copy(
                            a.pool0[:, hc * 640 + t * 128: hc * 640 + (t + 1) * 128],
                            pTs[:]), "sv")

                    rpts[pb] = sm.c["sv"]
                rbt[bt] = sm.c["stn"]
            s_bufT = sm.c["sv"]

            sgdone = {}
            for blk in range(16):
                fc, half = blk // 2, blk % 2
                pm = pms[blk % 4]

                @block.tensor
                def _(tensor, e=e, fc=fc, half=half, pm=pm, blk=blk, s_bufT=s_bufT):
                    if blk == 0:
                        sm.wait(tensor, "sin", wdone[(e, "g")])
                        sm.wait(tensor, "sv", s_bufT)
                    dom, cnt = rpm[blk % 4]
                    sm.wait(tensor, dom, cnt)
                    for hc in range(8):
                        ins = tensor.matmul(
                            pm[:], wbg[:, hc * D + fc * 128: hc * D + fc * 128 + 128],
                            a.pool0[:, hc * 640 + half * 320: hc * 640 + half * 320 + 320],
                            start=(hc == 0), stop=(hc == 7))
                    sm.inc(ins, "stn")

                t_mm = sm.c["stn"]

                @block.scalar
                def _(scalar, fc=fc, half=half, pm=pm, t_mm=t_mm):
                    sm.wait(scalar, "stn", t_mm)
                    sm.inc(scalar.activation(
                        a.pool1[:, fc * 640 + half * 320: fc * 640 + half * 320 + 320],
                        pm[:], AF.Silu), "ss")

                rpm[blk % 4] = ("ss", sm.c["ss"])
                sgdone[blk] = sm.c["ss"]
            wread[("buf", (e * 3 + 0) % 2)] = sm.c["stn"]
            emit_wload(e, "d", wd_d, (e * 3 + 2) % 2)

            sudone = {}
            for blk in range(16):
                fc, half = blk // 2, blk % 2
                pm = pms[blk % 4]

                @block.tensor
                def _(tensor, e=e, fc=fc, half=half, pm=pm, blk=blk):
                    if blk == 0:
                        sm.wait(tensor, "sin", wdone[(e, "u")])
                    dom, cnt = rpm[blk % 4]
                    sm.wait(tensor, dom, cnt)
                    for hc in range(8):
                        ins = tensor.matmul(
                            pm[:], wbu[:, hc * D + fc * 128: hc * D + fc * 128 + 128],
                            a.pool0[:, hc * 640 + half * 320: hc * 640 + half * 320 + 320],
                            start=(hc == 0), stop=(hc == 7))
                    sm.inc(ins, "stn")

                t_mm = sm.c["stn"]

                @block.vector
                def _(vector, fc=fc, half=half, pm=pm, t_mm=t_mm, blk=blk):
                    sm.wait(vector, "stn", t_mm)
                    sm.wait(vector, "ss", sgdone[blk])
                    sl = slice(fc * 640 + half * 320, fc * 640 + half * 320 + 320)
                    sm.inc(vector.tensor_tensor(a.pool2[:, sl], a.pool1[:, sl], pm[:],
                                                ALU.mult), "sv")

                rpm[blk % 4] = ("sv", sm.c["sv"])
                sudone[blk] = sm.c["sv"]
            wread[("buf", (e * 3 + 1) % 2)] = sm.c["stn"]

            ydone = {}
            for blk in range(16):
                hc, half = blk // 2, blk % 2
                pm = pms[blk % 4]

                @block.tensor
                def _(tensor, e=e, hc=hc, half=half, pm=pm, blk=blk):
                    if blk == 0:
                        sm.wait(tensor, "sin", wdone[(e, "d")])
                        sm.wait(tensor, "sv", max(sudone.values()))
                    dom, cnt = rpm[blk % 4]
                    sm.wait(tensor, dom, cnt)
                    for fc in range(8):
                        ins = tensor.matmul(
                            pm[:], wbd[:, fc * D + hc * 128: fc * D + hc * 128 + 128],
                            a.pool2[:, fc * 640 + half * 320: fc * 640 + half * 320 + 320],
                            start=(fc == 0), stop=(fc == 7))
                    sm.inc(ins, "stn")

                t_mm = sm.c["stn"]

                @block.scalar
                def _(scalar, hc=hc, half=half, pm=pm, t_mm=t_mm):
                    sm.wait(scalar, "stn", t_mm)
                    sm.inc(scalar.copy(
                        a.pool3[:, hc * 640 + half * 320: hc * 640 + half * 320 + 320],
                        pm[:]), "ss")

                rpm[blk % 4] = ("ss", sm.c["ss"])
                ydone[(hc, half)] = sm.c["ss"]
            wread[("buf", (e * 3 + 2) % 2)] = sm.c["stn"]
            wread[("pool0",)] = sm.c["stn"]

            for t in range(5):
                yb = t % 2
                ytile = (a.ytile, a.ytile2)[yb]
                for hc in range(8):
                    pb = hc % 2
                    pTs = (a.pTs, a.pTs1)[pb]
                    need = [ydone[(hc, 0)]]
                    if t * 128 + 128 > 320:
                        need.append(ydone[(hc, 1)])

                    @block.tensor
                    def _(tensor, t=t, hc=hc, pb=pb, pTs=pTs, need=need):
                        sm.wait(tensor, "ss", max(need))
                        sm.wait(tensor, "sv", rpts[pb])
                        sm.inc(tensor.transpose(a.pTs1[:] if pb else a.pTs[:],
                                                a.pool3[:, hc * 640 + t * 128: hc * 640 + (t + 1) * 128],
                                                a.ident[:]), "stn")

                    t_tr = sm.c["stn"]

                    @block.vector
                    def _(vector, t=t, hc=hc, pb=pb, pTs=pTs, t_tr=t_tr, yb=yb, ytile=ytile):
                        sm.wait(vector, "stn", t_tr)
                        if hc == 0:
                            sm.wait(vector, "sout", ryt[yb])
                        sm.inc(vector.tensor_copy(ytile[:, hc * 128:(hc + 1) * 128], pTs[:]), "sv")

                    rpts[pb] = sm.c["sv"]

                v_yt = sm.c["sv"]

                @block.gpsimd
                def _(gpsimd, e=e, t=t, v_yt=v_yt, ytile=ytile):
                    sm.wait(gpsimd, "sv", v_yt)
                    sm.inc(gpsimd.dma_start(
                        out=y4_d[e * CCAP + t * 128: e * CCAP + (t + 1) * 128, :],
                        in_=ytile[:]), "sout", 16)

                ryt[yb] = sm.c["sout"]

        @block.gpsimd
        def _(gpsimd):
            sm.wait(gpsimd, "sout")
    return nc


def kernel(**inputs):
    cores = list(range(NCORE))
    x = np.ascontiguousarray(inputs["L0_attn_x"].reshape(T, D))
    WT = np.ascontiguousarray(inputs["L0_attn_W"].T)
    knw = np.ascontiguousarray(np.tile(inputs["L0_attn_k_norm_w"], H))

    nc1 = build_l1()
    maps = [{"x": x[c * TC:(c + 1) * TC], "WT": WT, "knw": knw} for c in cores]
    r1 = run_bass_kernel_spmd(nc1, maps, core_ids=cores)
    static = np.concatenate([r["static"] for r in r1.results], axis=0)
    kmat = np.concatenate([r["kout"] for r in r1.results], axis=0)

    k_str = kmat.reshape(2, 8192, H, DH).transpose(0, 2, 1, 3).reshape(2 * H, 8192, DH)
    v_str = static.reshape(2, 8192, H, DH).transpose(0, 2, 1, 3).reshape(2 * H, 8192, DH)
    lam_l = inputs["L0_attn_lam_logit"]
    eta_r = inputs["L0_attn_eta_raw"]
    nc2 = build_l2(nch=64)
    maps = []
    for c in cores:
        sl = slice(c * 4, (c + 1) * 4)
        heads = [(c * 4 + i) % H for i in range(4)]
        maps.append({
            "k4": np.ascontiguousarray(k_str[sl].reshape(4, 64, 128, DH).transpose(1, 2, 0, 3)),
            "v4": np.ascontiguousarray(v_str[sl].reshape(4, 64, 128, DH).transpose(1, 2, 0, 3)),
            "lam4": np.ascontiguousarray(lam_l[heads]),
            "eta4": np.ascontiguousarray(eta_r[heads]),
        })
    r2 = run_bass_kernel_spmd(nc2, maps, core_ids=cores)
    o_str = np.stack([r["o4"].transpose(2, 0, 1, 3).reshape(4, 8192, DH)
                      for r in r2.results]).reshape(2 * H, 8192, DH)
    o_dyn = np.ascontiguousarray(
        o_str.reshape(2, H, 8192, DH).transpose(0, 2, 1, 3).reshape(T, D))

    nc3 = build_l3()
    maps = [{"x": x[c * TC:(c + 1) * TC], "st": static[c * TC:(c + 1) * TC],
             "od": o_dyn[c * TC:(c + 1) * TC], "ow": inputs["L0_attn_o_norm_w"],
             "ob": inputs["L0_attn_o_norm_b"], "wr": inputs["L0_ffn_w_router"]}
            for c in cores]
    r3 = run_bass_kernel_spmd(nc3, maps, core_ids=cores)
    resid1 = np.concatenate([r["resid"] for r in r3.results], axis=0)
    xt2 = np.concatenate([r["xt2o"] for r in r3.results], axis=0)
    logits = np.concatenate([r["lgo"] for r in r3.results], axis=0)

    nc5 = build_l5()
    wg, wu, wd = inputs["L0_ffn_w_gate"], inputs["L0_ffn_w_up"], inputs["L0_ffn_w_down"]
    maps = [{"lg": logits, "xt": xt2,
             "wg4": np.ascontiguousarray(wg[c * 4:(c + 1) * 4]),
             "wu4": np.ascontiguousarray(wu[c * 4:(c + 1) * 4]),
             "wd4": np.ascontiguousarray(wd[c * 4:(c + 1) * 4]),
             "soff": np.full((128, 1), c * 4 * CCAP, dtype=np.float32)}
            for c in cores]
    r5 = run_bass_kernel_spmd(nc5, maps, core_ids=cores)
    yfull = np.concatenate([r["y4"] for r in r5.results], axis=0)
    slot = r5.results[0]["slot"]
    gate = r5.results[0]["gate"]

    nc6 = build_l6()
    maps = [{"rz": resid1[c * TC:(c + 1) * TC], "xt2": xt2[c * TC:(c + 1) * TC],
             "yfull": yfull, "slot": slot[c * TC:(c + 1) * TC],
             "gate": gate[c * TC:(c + 1) * TC]} for c in cores]
    r6 = run_bass_kernel_spmd(nc6, maps, core_ids=cores)
    out = np.concatenate([r["out"] for r in r6.results], axis=0)
    return out.reshape(2, 8192, D)
